# version 03
# baseline (speedup 1.0000x reference)
import numpy as np
import ml_dtypes
from contextlib import ExitStack

import concourse.bass as bass
import concourse.mybir as mybir
from concourse.bass_utils import run_bass_kernel_spmd

F32 = mybir.dt.float32
BF16 = mybir.dt.bfloat16
I32 = mybir.dt.int32
ALU = mybir.AluOpType
AF = mybir.ActivationFunctionType
AX = mybir.AxisListType

D = 1024
S = 4096
B = 4
TC = 2048
NCH = 8
NCORE = 8
MAGIC = 12582912.0
BFNP = np.dtype(ml_dtypes.bfloat16)

ENGS = ("v", "s", "t", "g", "q")


class K:
    def __init__(self, nc):
        self.nc = nc
        self.plan = {e: [] for e in ENGS}
        self.tick = {e: 0 for e in ENGS}
        self.dtick = {e: 0 for e in ENGS}
        self.sems = {}
        self.dsems = {}
        self.sb = {}
        self.res = {}
        self._es = None

    def _r(self, name):
        if name not in self.res:
            self.res[name] = {"w": [], "r": []}
        return self.res[name]

    def _deps(self, reads, writes):
        toks = []
        for n in reads:
            toks += self._r(n)["w"]
        for n in writes:
            rr = self._r(n)
            toks += rr["w"] + rr["r"]
        best = {}
        for kind, key, val in toks:
            kk = (kind, key)
            if kk not in best or best[kk] < val:
                best[kk] = val
        return [(kind, key, val) for (kind, key), val in best.items()]

    def _note(self, tok, reads, writes):
        for n in reads:
            self._r(n)["r"].append(tok)
        for n in writes:
            self._r(n)["w"].append(tok)

    def _emit_waits(self, eng, deps):
        for kind, key, val in deps:
            eng.wait_ge(self.sems[key] if kind == "c" else self.dsems[key], val)

    def op(self, e, builder, r=(), w=(), extra=()):
        deps = self._deps(r, w) + list(extra)
        self.tick[e] += 1
        tgt = self.tick[e]

        def go(eng, _d=tuple(deps), _b=builder, _e=e):
            self._emit_waits(eng, _d)
            _b(eng).then_inc(self.sems[_e], 1)
        self.plan[e].append(go)
        tok = ("c", e, tgt)
        self._note(tok, r, w)
        return tok

    def multi(self, e, builder, r=(), w=(), extra=()):
        deps = self._deps(r, w) + list(extra)
        self.tick[e] += 1
        tgt = self.tick[e]

        def go(eng, _d=tuple(deps), _b=builder, _e=e):
            self._emit_waits(eng, _d)
            _b(eng).then_inc(self.sems[_e], 1)
        self.plan[e].append(go)
        tok = ("c", e, tgt)
        self._note(tok, r, w)
        return tok

    def dma(self, e, out, in_, r=(), w=(), extra=()):
        deps = self._deps(r, w) + list(extra)
        self.dtick[e] += 16
        tgt = self.dtick[e]

        def go(eng, _d=tuple(deps), _o=out, _i=in_, _e=e):
            self._emit_waits(eng, _d)
            eng.dma_start(out=_o, in_=_i).then_inc(self.dsems[_e], 16)
        self.plan[e].append(go)
        tok = ("d", e, tgt)
        self._note(tok, r, w)
        return tok

    def raw(self, e, fn, r=(), w=(), extra=()):
        deps = self._deps(r, w) + list(extra)

        def go(eng, _d=tuple(deps), _f=fn):
            self._emit_waits(eng, _d)
            _f(eng)
        self.plan[e].append(go)

    def fence(self, e, tokens):
        def go(eng, _d=tuple(tokens)):
            self._emit_waits(eng, _d)
        self.plan[e].append(go)

    def tokens_of(self, *names):
        out = []
        for n in names:
            rr = self._r(n)
            out += rr["w"] + rr["r"]
        best = {}
        for kind, key, val in out:
            kk = (kind, key)
            if kk not in best or best[kk] < val:
                best[kk] = val
        return [(kind, key, val) for (kind, key), val in best.items()]

    def sbuf(self, name, shape, dtype):
        h = self._es.enter_context(self.nc.sbuf_tensor(name, list(shape), dtype))
        self.sb[name] = h
        return h

    def psum(self, name, shape, dtype=F32):
        h = self._es.enter_context(self.nc.psum_tensor(name, list(shape), dtype))
        self.sb[name] = h
        return h

    def start(self, es):
        self._es = es
        es.enter_context(self.nc.allow_low_precision(
            reason="bf16 matmul operands / quantized values are intentional"))
        for e in ENGS:
            self.sems[e] = es.enter_context(self.nc.semaphore(f"sem_{e}"))
            self.dsems[e] = es.enter_context(self.nc.semaphore(f"dsem_{e}"))

    def finalize(self):
        block = self._es.enter_context(self.nc.Block())
        plans = self.plan

        def runner(key):
            def run(eng):
                for fn in plans[key]:
                    fn(eng)
            return run
        block.vector(runner("v"))
        block.scalar(runner("s"))
        block.tensor(runner("t"))
        block.gpsimd(runner("g"))
        block.sync(runner("q"))


def alloc_env(k):
    E = {}
    E["XR"] = k.sbuf("XR", [128, NCH * TC], F32)
    E["LNB"] = k.sbuf("LNB", [128, NCH * S], BF16)
    E["WS"] = k.sbuf("WS", [128, 2048], F32)
    E["WB"] = k.sbuf("WB", [128, 4096], BF16)
    E["SC"] = k.sbuf("SC", [128, 2048], F32)
    E["BC"] = k.sbuf("BC", [128, 1024], F32)
    E["SB2"] = k.sbuf("SB2", [128, 1536], BF16)
    E["CT"] = k.sbuf("CT", [128, 32], F32)
    E["IDB"] = k.sbuf("IDB", [128, 128], BF16)
    E["IDF"] = k.sbuf("IDF", [128, 128], F32)
    E["KONF"] = k.sbuf("KONF", [128, 8], F32)
    E["SCL"] = k.sbuf("SCL", [128, 224], F32)
    E["ONE1F"] = k.sbuf("ONE1F", [1, 128], F32)
    E["ONE1B"] = k.sbuf("ONE1B", [1, 128], BF16)
    AR = k.sbuf("AR", [128, 8768], BF16)
    E["AR"] = AR
    E["TMX"] = AR[:, 8256:8768]
    E["KA"] = AR[:, 0:4096]
    E["QA"] = AR[:, 4096:6144]
    E["VT"] = AR[:, 6144:8256]
    E["LORA"] = AR[:, 8256:8768]
    E["BA"] = k.sbuf("BA", [128, 128], F32)
    E["BAH"] = k.sbuf("BAH", [128, 128], F32)
    E["ONESB"] = k.sbuf("ONESB", [128, 1], BF16)
    E["SELB"] = k.sbuf("SELB", [1, 2], BF16)
    E["R1"] = k.sbuf("R1", [1, 2048], F32)
    E["R4"] = k.sbuf("R4", [1, 2048], F32)
    E["R2"] = k.sbuf("R2", [1, 2048], BF16)
    E["R3"] = k.sbuf("R3", [1, 2048], BF16)
    E["R5"] = k.sbuf("R5", [1, 2048], BF16)
    for i in range(6):
        E[f"ps{i}"] = k.psum(f"ps{i}", [128, 512], F32)
    E["psB"] = k.psum("psB", [128, 512], F32)
    E["psT"] = k.psum("psT", [128, 1024], BF16)
    return E


def kc(E, i):
    return E["KONF"][:, i:i + 1]


def emit_env_init(k, E, kon_ap):
    t = k.dma("q", E["WS"][:, 0:140], kon_ap, w=("WS",))
    k.op("v", lambda e: e.tensor_copy(E["IDB"][:], E["WS"][:, 0:128]), r=("WS",), w=("IDB",))
    k.op("v", lambda e: e.tensor_copy(E["IDF"][:], E["WS"][:, 0:128]), r=("WS",), w=("IDF",))
    k.op("v", lambda e: e.tensor_copy(E["KONF"][:, 0:6], E["WS"][:, 128:134]), r=("WS",), w=("KONF",))
    k.op("v", lambda e: e.memset(E["ONE1F"][:], 1.0), w=("ONE1F",))
    k.op("v", lambda e: e.memset(E["ONE1B"][:], 1.0), w=("ONE1B",))
    k.op("v", lambda e: e.memset(E["ONESB"][:], 1.0), w=("ONESB",))
    k.op("v", lambda e: e.memset(E["SELB"][0:1, 0:1], 0.0), w=("SELB",))
    k.op("v", lambda e: e.memset(E["SELB"][0:1, 1:2], 1.0), w=("SELB",))
    k.op("v", lambda e: e.memset(E["CT"][:], 0.0), w=("CT",))


def emit_ln(k, E, src, dst, tspan, tags_src, tags_dst, gain_bias=None,
            src_bf16=False, keep_stats=False, gb_tags=()):
    ones = E["ONESB"][:, 0:1] if src_bf16 else kc(E, 0)
    NT = (tspan + 511) // 512
    for tt in range(NT):
        sl = slice(tt * 512, min(tspan, (tt + 1) * 512))
        n = sl.stop - sl.start

        def mm_sum(eng, _sl=sl, _n=n):
            for ci in range(NCH):
                inst = eng.matmul(E["psB"][0:1, 0:_n], ones, src[ci][:, _sl],
                                  start=(ci == 0), stop=(ci == NCH - 1))
            return inst
        k.multi("t", mm_sum, r=tuple(tags_src) + ("KONF", "ONESB"), w=("psB",))
        k.op("v", lambda e, _sl=sl, _n=n: e.tensor_scalar(
            E["R1"][0:1, _sl], E["psB"][0:1, 0:_n], 1.0 / D, None, ALU.mult),
            r=("psB",), w=("R1",))
        for ci in range(NCH):
            k.op("s", lambda e, _ci=ci, _sl=sl, _n=n: e.activation(
                E["SC"][:, 0:_n], src[_ci][:, _sl], AF.Square),
                r=tuple(tags_src), w=("SC",))
            k.multi("t", lambda e, _ci=ci, _n=n: e.matmul(
                E["psB"][0:1, 0:_n], kc(E, 0), E["SC"][:, 0:_n],
                start=(_ci == 0), stop=(_ci == NCH - 1)),
                r=("SC", "KONF"), w=("psB",))
        k.op("v", lambda e, _sl=sl, _n=n: e.tensor_scalar(
            E["R4"][0:1, _sl], E["psB"][0:1, 0:_n], 1.0 / D, None, ALU.mult),
            r=("psB",), w=("R4",))
    k.op("v", lambda e: e.scalar_tensor_tensor(
        E["SC"][0:1, 0:tspan], E["R1"][0:1, 0:tspan], -1.0,
        E["R1"][0:1, 0:tspan], ALU.mult, ALU.mult), r=("R1",), w=("SC",))
    k.op("v", lambda e: e.tensor_add(
        E["R4"][0:1, 0:tspan], E["R4"][0:1, 0:tspan], E["SC"][0:1, 0:tspan]),
        r=("R4", "SC"), w=("R4",))
    k.op("s", lambda e: e.activation(
        E["R4"][0:1, 0:tspan], E["R4"][0:1, 0:tspan], AF.Ln,
        bias=kc(E, 1)[0:1, :], scale=1.0), r=("R4", "KONF"), w=("R4",))
    if keep_stats:
        k.op("s", lambda e: e.activation(
            E["R4"][0:1, 0:tspan], E["R4"][0:1, 0:tspan], AF.Exp,
            scale=kc(E, 2)[0:1, :]), r=("R4", "KONF"), w=("R4",))
        k.op("v", lambda e: e.tensor_copy(
            E["R2"][0:1, 0:tspan], E["R4"][0:1, 0:tspan]), r=("R4",), w=("R2",))
    else:
        k.op("s", lambda e: e.activation(
            E["R2"][0:1, 0:tspan], E["R4"][0:1, 0:tspan], AF.Exp,
            scale=kc(E, 2)[0:1, :]), r=("R4", "KONF"), w=("R2",))
    k.op("v", lambda e: e.scalar_tensor_tensor(
        E["R3"][0:1, 0:tspan], E["R1"][0:1, 0:tspan], -1.0,
        E["R2"][0:1, 0:tspan], ALU.mult, ALU.mult), r=("R1", "R2"), w=("R3",))
    for tt in range(NT):
        sl = slice(tt * 512, min(tspan, (tt + 1) * 512))
        n = sl.stop - sl.start
        emit_bcast_row(k, E, E["R2"][0:1, sl], "R2", E["BC"][:, 0:n], "BC", n)
        emit_bcast_row(k, E, E["R3"][0:1, sl], "R3", E["BC"][:, 512:512 + n], "BC", n)
        for ci in range(NCH):
            k.op("v", lambda e, _ci=ci, _sl=sl, _n=n: e.tensor_mul(
                E["SC"][:, 0:_n], src[_ci][:, _sl], E["BC"][:, 0:_n]),
                r=tuple(tags_src) + ("BC",), w=("SC",))
            if gain_bias is None:
                k.op("v", lambda e, _ci=ci, _sl=sl, _n=n: e.tensor_add(
                    dst[_ci][:, _sl], E["SC"][:, 0:_n], E["BC"][:, 512:512 + _n]),
                    r=("SC", "BC"), w=tuple(tags_dst))
            else:
                gch, bch = gain_bias
                k.op("v", lambda e, _sl=sl, _n=n: e.tensor_add(
                    E["SC"][:, 0:_n], E["SC"][:, 0:_n], E["BC"][:, 512:512 + _n]),
                    r=("SC", "BC"), w=("SC",))
                k.op("v", lambda e, _ci=ci, _sl=sl, _n=n: e.tensor_scalar(
                    dst[_ci][:, _sl], E["SC"][:, 0:_n], gch[_ci], bch[_ci],
                    ALU.mult, ALU.add), r=("SC",) + tuple(gb_tags),
                    w=tuple(tags_dst))


def emit_bcast_row(k, E, row_ap, row_tag, dst_ap, dst_tag, n):
    ones = E["ONE1B"][0:1, :] if row_ap.dtype == BF16 else E["ONE1F"][0:1, :]
    k.multi("t", lambda e: e.matmul(E["psB"][0:128, 0:n], ones, row_ap,
                                    start=True, stop=True),
            r=(row_tag, "ONE1F", "ONE1B"), w=("psB",))
    k.op("s", lambda e: e.copy(dst_ap, E["psB"][0:128, 0:n]), r=("psB",), w=(dst_tag,))


def emit_token_absmax(k, E, chunks, tspan, tags, row_name, clip_lo, t0=0):
    for tt in range((tspan + 511) // 512):
        sl = slice(tt * 512, min(tspan, (tt + 1) * 512))
        n = sl.stop - sl.start
        for ci, ap in enumerate(chunks):
            if ci == 0:
                k.op("s", lambda e, _a=ap, _sl=sl, _n=n: e.activation(
                    E["TMX"][:, 0:_n], _a[:, _sl], AF.Abs),
                    r=tuple(tags), w=("TMX",))
            else:
                k.op("s", lambda e, _a=ap, _sl=sl, _n=n: e.activation(
                    E["SB2"][:, 0:_n], _a[:, _sl], AF.Abs),
                    r=tuple(tags), w=("SB2",))
                k.op("v", lambda e, _n=n: e.tensor_max(
                    E["TMX"][:, 0:_n], E["TMX"][:, 0:_n], E["SB2"][:, 0:_n]),
                    r=("SB2", "TMX"), w=("TMX",))
        emit_colmax_to_row(k, E, E["TMX"], "TMX", 0, n, row_name, t0 + sl.start,
                           clip_lo)


def emit_colmax_to_row(k, E, srctile, srctag, c0, tspan, row_name, t0, clip_lo):
    row = E[row_name]
    NTT = tspan // 128
    for tt in range(NTT):
        k.multi("t", lambda e, _tt=tt: e.transpose(
            E["psT"][0:128, 0:128],
            srctile[:, c0 + _tt * 128:c0 + (_tt + 1) * 128], E["IDB"][:]),
            r=(srctag, "IDB"), w=("psT",))
        k.op("v", lambda e, _tt=tt: e.reduce_max(
            E["CT"][:, 20 + (_tt % 4):21 + (_tt % 4)], E["psT"][0:128, 0:128],
            axis=AX.X), r=("psT",), w=("CT",))
        k.op("v", lambda e, _tt=tt: e.tensor_scalar(
            E["CT"][:, 20 + (_tt % 4):21 + (_tt % 4)],
            E["CT"][:, 20 + (_tt % 4):21 + (_tt % 4)], clip_lo, None, ALU.max),
            r=("CT",), w=("CT",))
        k.multi("t", lambda e, _tt=tt: e.transpose(
            E["ps5"][0:1, 0:128], E["CT"][:, 20 + (_tt % 4):21 + (_tt % 4)],
            E["IDF"][:]), r=("CT", "IDF"), w=("ps5",))
        k.op("v", lambda e, _tt=tt: e.tensor_copy(
            row[0:1, t0 + _tt * 128:t0 + (_tt + 1) * 128], E["ps5"][0:1, 0:128]),
            r=("ps5",), w=(row_name,))


def emit_weight_quant(k, E, wT_ap, q_ap, kchunks, ocols, ctcol):
    n_el = float(kchunks * 128 * ocols)
    nh = (ocols + 1535) // 1536
    for kci in range(kchunks):
        for hh in range(nh):
            c0, c1 = hh * 1536, min(ocols, (hh + 1) * 1536)
            k.dma("q", E["WS"][:, 0:c1 - c0],
                  wT_ap[kci * 128:(kci + 1) * 128, c0:c1], w=("WS",))
            if kci == 0 and hh == 0:
                k.op("v", lambda e, _n=c1 - c0: e.reduce_sum(
                    E["CT"][:, 16:17], E["WS"][:, 0:_n], axis=AX.X,
                    apply_absolute_value=True), r=("WS",), w=("CT",))
            else:
                k.op("v", lambda e, _n=c1 - c0: e.reduce_sum(
                    E["CT"][:, 17:18], E["WS"][:, 0:_n], axis=AX.X,
                    apply_absolute_value=True), r=("WS",), w=("CT",))
                k.op("v", lambda e: e.tensor_add(
                    E["CT"][:, 16:17], E["CT"][:, 16:17], E["CT"][:, 17:18]),
                    r=("CT",), w=("CT",))
    k.multi("t", lambda e: e.matmul(E["psB"][0:1, 0:1], E["CT"][:, 16:17], kc(E, 0),
                                    start=True, stop=True), r=("CT", "KONF"), w=("psB",))
    k.op("v", lambda e: e.tensor_scalar(
        E["CT"][0:1, ctcol:ctcol + 1], E["psB"][0:1, 0:1], 1.0 / n_el, 1e-5,
        ALU.mult, ALU.max), r=("psB",), w=("CT",))
    k.op("v", lambda e: e.reciprocal(E["CT"][0:1, ctcol + 1:ctcol + 2],
                                     E["CT"][0:1, ctcol:ctcol + 1]), r=("CT",), w=("CT",))
    k.multi("t", lambda e: e.matmul(E["psB"][0:128, 0:1], E["ONE1F"][0:1, :],
                                    E["CT"][0:1, ctcol + 1:ctcol + 2], start=True,
                                    stop=True), r=("CT", "ONE1F"), w=("psB",))
    k.op("s", lambda e: e.copy(E["CT"][:, 18:19], E["psB"][0:128, 0:1]),
         r=("psB",), w=("CT",))
    for kci in range(kchunks):
        for hh in range(nh):
            c0, c1 = hh * 1536, min(ocols, (hh + 1) * 1536)
            nn = c1 - c0
            k.dma("q", E["WS"][:, 0:nn],
                  wT_ap[kci * 128:(kci + 1) * 128, c0:c1], w=("WS",))
            k.op("s", lambda e, _nn=nn: e.activation(
                E["SC"][:, 0:_nn], E["WS"][:, 0:_nn], AF.Copy,
                scale=E["CT"][:, 18:19]), r=("WS", "CT"), w=("SC",))
            k.op("v", lambda e, _nn=nn: e.tensor_scalar(
                E["SC"][:, 0:_nn], E["SC"][:, 0:_nn], MAGIC, MAGIC,
                ALU.add, ALU.subtract), r=("SC",), w=("SC",))
            k.op("v", lambda e, _nn=nn: e.tensor_scalar(
                E["SB2"][:, 0:_nn], E["SC"][:, 0:_nn], -1.0, 1.0,
                ALU.max, ALU.min), r=("SC",), w=("SB2",))
            k.dma("g", q_ap[kci * 128:(kci + 1) * 128, c0:c1], E["SB2"][:, 0:nn],
                  r=("SB2",), w=("HBMQ",))


def emit_l0_ffn(k, E, io):
    xr = [E["XR"][:, c * TC:(c + 1) * TC] for c in range(NCH)]
    ln = [E["LNB"][:, c * S:c * S + TC] for c in range(NCH)]
    lntags = tuple(f"LN{c}" for c in range(NCH))
    xrtags = tuple(f"XR{c}" for c in range(NCH))

    def hslot(j):
        base = (j // 4) * S + TC + (j % 4) * 512
        return E["LNB"][:, base:base + 512]
    htags = tuple(f"H{j}" for j in range(32))

    emit_ln(k, E, xr, ln, TC, xrtags, lntags)

    emit_weight_quant(k, E, io["wgT"], io["qg"], 8, 4096, 24)
    emit_weight_quant(k, E, io["wuT"], io["qu"], 8, 4096, 26)
    emit_weight_quant(k, E, io["wdT"], io["qd"], 32, 1024, 28)

    emit_token_absmax(k, E, ln, TC, lntags, "R1", 1e-5)
    k.op("v", lambda e: e.reciprocal(E["R2"][0:1, 0:TC], E["R1"][0:1, 0:TC]),
         r=("R1",), w=("R2",))
    k.op("v", lambda e: e.tensor_scalar(
        E["R2"][0:1, 0:TC], E["R2"][0:1, 0:TC], 127.0, None, ALU.mult),
        r=("R2",), w=("R2",))
    k.op("v", lambda e: e.tensor_scalar(
        E["R1"][0:1, 0:TC], E["R1"][0:1, 0:TC], 1.0 / 127.0, None, ALU.mult),
        r=("R1",), w=("R1",))
    for tt in range(4):
        sl = slice(tt * 512, (tt + 1) * 512)
        emit_bcast_row(k, E, E["R2"][0:1, sl], "R2", E["BC"][:, 0:512], "BC", 512)
        for ci in range(NCH):
            k.op("v", lambda e, _ci=ci, _sl=sl: e.tensor_mul(
                E["SC"][:, 0:512], ln[_ci][:, _sl], E["BC"][:, 0:512]),
                r=lntags + ("BC",), w=("SC",))
            k.op("v", lambda e, _ci=ci, _sl=sl: e.tensor_scalar(
                ln[_ci][:, _sl], E["SC"][:, 0:512], MAGIC, MAGIC, ALU.add,
                ALU.subtract), r=("SC",), w=lntags)
    k.op("v", lambda e: e.tensor_scalar(
        E["R3"][0:1, 0:TC], E["R1"][0:1, 0:TC], E["CT"][0:1, 24:25],
        None, ALU.mult), r=("R1", "CT"), w=("R3",))
    k.op("v", lambda e: e.tensor_scalar(
        E["R5"][0:1, 0:TC], E["R1"][0:1, 0:TC], E["CT"][0:1, 26:27],
        None, ALU.mult), r=("R1", "CT"), w=("R5",))

    for tt in range(4):
        sl = slice(tt * 512, (tt + 1) * 512)
        emit_bcast_row(k, E, E["R3"][0:1, sl], "R3", E["BC"][:, 0:512], "BC", 512)
        emit_bcast_row(k, E, E["R5"][0:1, sl], "R5", E["BC"][:, 512:1024], "BC", 512)
        for og in range(16):
            oc0 = og * 256
            for kk in range(8):
                k.dma("q", E["WB"][:, kk * 256:(kk + 1) * 256],
                      io["qg"][kk * 128:(kk + 1) * 128, oc0:oc0 + 256],
                      r=("HBMQ",), w=("WB",))
                k.dma("g", E["WB"][:, 2048 + kk * 256:2048 + (kk + 1) * 256],
                      io["qu"][kk * 128:(kk + 1) * 128, oc0:oc0 + 256],
                      r=("HBMQ",), w=("WB",))
            for oc in range(2):
                j = og * 2 + oc

                def mm_g(eng, _oc=oc, _sl=sl):
                    for kk in range(8):
                        inst = eng.matmul(
                            E["ps0"][0:128, 0:512],
                            E["WB"][:, kk * 256 + _oc * 128:kk * 256 + _oc * 128 + 128],
                            ln[kk][:, _sl], start=(kk == 0), stop=(kk == 7))
                    return inst

                def mm_u(eng, _oc=oc, _sl=sl):
                    for kk in range(8):
                        inst = eng.matmul(
                            E["ps1"][0:128, 0:512],
                            E["WB"][:, 2048 + kk * 256 + _oc * 128:2048 + kk * 256 + _oc * 128 + 128],
                            ln[kk][:, _sl], start=(kk == 0), stop=(kk == 7))
                    return inst
                k.multi("t", mm_g, r=("WB",) + lntags, w=("ps0",))
                k.multi("t", mm_u, r=("WB",) + lntags, w=("ps1",))
                k.op("v", lambda e: e.tensor_mul(
                    E["SC"][:, 0:512], E["ps0"][0:128, 0:512], E["BC"][:, 0:512]),
                    r=("ps0", "BC"), w=("SC",))
                k.op("s", lambda e: e.activation(
                    E["SC"][:, 512:1024], E["SC"][:, 0:512], AF.Silu),
                    r=("SC",), w=("SC",))
                k.op("v", lambda e: e.tensor_mul(
                    E["SC"][:, 1024:1536], E["ps1"][0:128, 0:512],
                    E["BC"][:, 512:1024]), r=("ps1", "BC"), w=("SC",))
                hj = hslot(j)
                k.op("v", lambda e, _h=hj: e.tensor_mul(
                    _h, E["SC"][:, 512:1024], E["SC"][:, 1024:1536]),
                    r=("SC",), w=(f"H{j}",))
                if j == 0:
                    k.op("s", lambda e, _h=hj: e.activation(
                        E["TMX"][:, 0:512], _h, AF.Abs), r=(f"H{j}",), w=("TMX",))
                else:
                    k.op("s", lambda e, _h=hj: e.activation(
                        E["SB2"][:, 0:512], _h, AF.Abs), r=(f"H{j}",), w=("SB2",))
                    k.op("v", lambda e: e.tensor_max(
                        E["TMX"][:, 0:512], E["TMX"][:, 0:512], E["SB2"][:, 0:512]),
                        r=("SB2", "TMX"), w=("TMX",))
        emit_colmax_to_row(k, E, E["TMX"], "TMX", 0, 512, "R4", sl.start, 1e-5)
        k.op("v", lambda e, _sl=sl: e.reciprocal(
            E["R2"][0:1, _sl], E["R4"][0:1, _sl]), r=("R4",), w=("R2",))
        k.op("v", lambda e, _sl=sl: e.tensor_scalar(
            E["R2"][0:1, _sl], E["R2"][0:1, _sl], 127.0, None, ALU.mult),
            r=("R2",), w=("R2",))
        k.op("v", lambda e, _sl=sl: e.tensor_scalar(
            E["R4"][0:1, _sl], E["R4"][0:1, _sl], E["CT"][0:1, 28:29],
            1.0 / 127.0, ALU.mult, ALU.mult), r=("R4", "CT"), w=("R4",))
        emit_bcast_row(k, E, E["R2"][0:1, sl], "R2", E["BC"][:, 0:512], "BC", 512)
        for j in range(32):
            hj = hslot(j)
            k.op("v", lambda e, _h=hj: e.tensor_mul(
                E["SC"][:, 0:512], _h, E["BC"][:, 0:512]),
                r=(f"H{j}", "BC"), w=("SC",))
            k.op("v", lambda e, _h=hj: e.tensor_scalar(
                _h, E["SC"][:, 0:512], MAGIC, MAGIC, ALU.add, ALU.subtract),
                r=("SC",), w=(f"H{j}",))
        emit_bcast_row(k, E, E["R4"][0:1, sl], "R4", E["BC"][:, 512:1024], "BC", 512)
        for ocd in range(8):
            for j2 in range(8):
                k.dma("q" if j2 % 2 == 0 else "g",
                      E["WB"][:, j2 * 512:(j2 + 1) * 512].rearrange(
                          "p (a o) -> p a o", a=4),
                      io["qd"][j2 * 512:(j2 + 1) * 512,
                               ocd * 128:(ocd + 1) * 128].rearrange(
                          "(a p) o -> p a o", p=128), r=("HBMQ",), w=("WB",))

            def mm_d(eng):
                for j3 in range(32):
                    inst = eng.matmul(
                        E["ps2"][0:128, 0:512],
                        E["WB"][:, j3 * 128:(j3 + 1) * 128], hslot(j3),
                        start=(j3 == 0), stop=(j3 == 31))
                return inst
            k.multi("t", mm_d, r=("WB",) + htags, w=("ps2",))
            k.op("v", lambda e: e.tensor_mul(
                E["SC"][:, 0:512], E["ps2"][0:128, 0:512], E["BC"][:, 512:1024]),
                r=("ps2", "BC"), w=("SC",))
            k.op("v", lambda e, _ocd=ocd, _sl=sl: e.tensor_add(
                xr[_ocd][:, _sl], xr[_ocd][:, _sl], E["SC"][:, 0:512]),
                r=("SC", f"XR{ocd}"), w=(f"XR{ocd}",))


def _t(a):
    return np.ascontiguousarray(np.asarray(a, dtype=np.float32).T)


def _c(a):
    return np.ascontiguousarray(np.asarray(a, dtype=np.float32))


def konst_host():
    kon = np.zeros((128, 140), np.float32)
    kon[:, 0:128] = np.eye(128, dtype=np.float32)
    kon[:, 128] = 1.0
    kon[:, 129] = 1e-5
    kon[:, 130] = -0.5
    kon[:, 131] = 1.0
    kon[:, 132] = 0.0625
    kon[:, 133] = -1.0
    return kon


def emit_load_resid(k, E, x_ap):
    for c in range(NCH):
        k.dma("q", E["XR"][:, c * TC:(c + 1) * TC], x_ap[c * 128:(c + 1) * 128, :],
              w=(f"XR{c}",))


def emit_store_resid(k, E, y_ap):
    toks = []
    for c in range(NCH):
        toks.append(k.dma("g", y_ap[c * 128:(c + 1) * 128, :],
                          E["XR"][:, c * TC:(c + 1) * TC], r=(f"XR{c}",)))
    k.fence("g", toks)


def build_l0_ffn_program():
    nc = bass.Bass("TRN2", target_bir_lowering=False, debug=False,
                   detect_race_conditions=False)
    x_in = nc.dram_tensor("x_in", [D, TC], F32, kind="ExternalInput").ap()
    kon = nc.dram_tensor("kon", [128, 140], F32, kind="ExternalInput").ap()
    wgT = nc.dram_tensor("wgT", [1024, 4096], F32, kind="ExternalInput").ap()
    wuT = nc.dram_tensor("wuT", [1024, 4096], F32, kind="ExternalInput").ap()
    wdT = nc.dram_tensor("wdT", [4096, 1024], F32, kind="ExternalInput").ap()
    x_out = nc.dram_tensor("x_out", [D, TC], F32, kind="ExternalOutput").ap()
    qg = nc.dram_tensor("qg", [1024, 4096], BF16, kind="Internal").ap()
    qu = nc.dram_tensor("qu", [1024, 4096], BF16, kind="Internal").ap()
    qd = nc.dram_tensor("qd", [4096, 1024], BF16, kind="Internal").ap()
    es = ExitStack()
    k = K(nc)
    with es:
        k.start(es)
        E = alloc_env(k)
        emit_env_init(k, E, kon)
        emit_load_resid(k, E, x_in)
        emit_l0_ffn(k, E, {"wgT": wgT, "wuT": wuT, "wdT": wdT,
                           "qg": qg, "qu": qu, "qd": qd})
        emit_store_resid(k, E, x_out)
        k.finalize()
    return nc


def run_l0_ffn(x_cores, inputs):
    nc = build_l0_ffn_program()
    kon = konst_host()
    wgT = _t(inputs["L0_ffn_w_gate"])
    wuT = _t(inputs["L0_ffn_w_up"])
    wdT = _t(inputs["L0_ffn_w_down"])
    in_maps = [{"x_in": x_cores[c], "kon": kon, "wgT": wgT, "wuT": wuT, "wdT": wdT}
               for c in range(NCORE)]
    res = run_bass_kernel_spmd(nc, in_maps, core_ids=list(range(NCORE)))
    return [res.results[c]["x_out"] for c in range(NCORE)]


def kernel(**inputs):
    x = np.asarray(inputs["L0_attn_x"], np.float32)
    x_cores = [np.ascontiguousarray(x[c // 2, (c % 2) * TC:((c % 2) + 1) * TC, :].T)
               for c in range(NCORE)]
    x_cores = run_attn("L0", x_cores, inputs)
    x_cores = run_l0_ffn(x_cores, inputs)
    x_cores = run_attn("L1", x_cores, inputs)
    x_cores = run_l1_ffn(x_cores, inputs)
    x_cores = run_attn("L2", x_cores, inputs)
    x_cores = run_l2_ffn(x_cores, inputs)
    x_cores = run_l3a(x_cores, inputs)
    x_cores = run_l3_ffn(x_cores, inputs)
    out = np.empty((B, S, D), np.float32)
    for c in range(NCORE):
        out[c // 2, (c % 2) * TC:((c % 2) + 1) * TC, :] = x_cores[c].T
    return out


def emit_w_to_bf16(k, E, wT_ap, q_ap, kchunks, ocols):
    nh = (ocols + 1535) // 1536
    for kci in range(kchunks):
        for hh in range(nh):
            c0, c1 = hh * 1536, min(ocols, (hh + 1) * 1536)
            nn = c1 - c0
            k.dma("q", E["WS"][:, 0:nn], wT_ap[kci * 128:(kci + 1) * 128, c0:c1],
                  w=("WS",))
            k.op("v", lambda e, _nn=nn: e.tensor_copy(
                E["SB2"][:, 0:_nn], E["WS"][:, 0:_nn]), r=("WS",), w=("SB2",))
            k.dma("g", q_ap[kci * 128:(kci + 1) * 128, c0:c1], E["SB2"][:, 0:nn],
                  r=("SB2",), w=("HBMQ",))


def emit_l2_ffn(k, E, io):
    xr = [E["XR"][:, c * TC:(c + 1) * TC] for c in range(NCH)]
    ln = [E["LNB"][:, c * S:c * S + TC] for c in range(NCH)]
    lntags = tuple(f"LN{c}" for c in range(NCH))
    xrtags = tuple(f"XR{c}" for c in range(NCH))

    def hslot(j):
        base = (j // 4) * S + TC + (j % 4) * 512
        return E["LNB"][:, base:base + 512]
    htags = tuple(f"H{j}" for j in range(32))

    k.dma("q", E["SCL"][:, 0:136], io["scl"], w=("SCL",))
    emit_ln(k, E, xr, ln, TC, xrtags, lntags)
    emit_w_to_bf16(k, E, io["w1T"], io["q1"], 8, 4096)
    emit_w_to_bf16(k, E, io["w2T"], io["q2"], 8, 4096)
    emit_w_to_bf16(k, E, io["h1T"], io["qh1"], 32, 4096)
    emit_w_to_bf16(k, E, io["h2T"], io["qh2"], 32, 4096)
    emit_w_to_bf16(k, E, io["w3T"], io["q3"], 32, 1024)
    for tt in range(4):
        sl = slice(tt * 512, (tt + 1) * 512)
        for og in range(16):
            for kk in range(8):
                k.dma("q", E["WB"][:, kk * 256:(kk + 1) * 256],
                      io["q1"][kk * 128:(kk + 1) * 128, og * 256:(og + 1) * 256],
                      r=("HBMQ",), w=("WB",))
                k.dma("g", E["WB"][:, 2048 + kk * 256:2048 + (kk + 1) * 256],
                      io["q2"][kk * 128:(kk + 1) * 128, og * 256:(og + 1) * 256],
                      r=("HBMQ",), w=("WB",))
            for oc in range(2):
                j = og * 2 + oc

                def mm_1(eng, _oc=oc, _sl=sl):
                    for kk in range(8):
                        inst = eng.matmul(
                            E["ps0"][0:128, 0:512],
                            E["WB"][:, kk * 256 + _oc * 128:kk * 256 + _oc * 128 + 128],
                            ln[kk][:, _sl], start=(kk == 0), stop=(kk == 7))
                    return inst

                def mm_2(eng, _oc=oc, _sl=sl):
                    for kk in range(8):
                        inst = eng.matmul(
                            E["ps1"][0:128, 0:512],
                            E["WB"][:, 2048 + kk * 256 + _oc * 128:2048 + kk * 256 + _oc * 128 + 128],
                            ln[kk][:, _sl], start=(kk == 0), stop=(kk == 7))
                    return inst
                k.multi("t", mm_1, r=("WB",) + lntags, w=("ps0",))
                k.multi("t", mm_2, r=("WB",) + lntags, w=("ps1",))
                k.op("s", lambda e, _j=j: e.activation(
                    E["SC"][:, 0:512], E["ps0"][0:128, 0:512], AF.Silu,
                    scale=E["SCL"][:, _j:_j + 1]), r=("ps0", "SCL"), w=("SC",))
                k.op("s", lambda e, _j=j: e.activation(
                    E["SC"][:, 512:1024], E["ps1"][0:128, 0:512], AF.Copy,
                    scale=E["SCL"][:, 32 + _j:33 + _j]), r=("ps1", "SCL"), w=("SC",))
                k.op("v", lambda e, _h=None, _j=j: e.tensor_mul(
                    hslot(_j), E["SC"][:, 0:512], E["SC"][:, 512:1024]),
                    r=("SC",), w=(f"H{j}",))
        for oc2 in range(32):
            for j2 in range(8):
                k.dma("q" if j2 % 2 == 0 else "g",
                      E["WB"][:, j2 * 512:(j2 + 1) * 512].rearrange(
                          "p (a o) -> p a o", a=4),
                      io["qh1"][j2 * 512:(j2 + 1) * 512,
                                oc2 * 128:(oc2 + 1) * 128].rearrange(
                          "(a p) o -> p a o", p=128), r=("HBMQ",), w=("WB",))

            def mm_h1(eng):
                for j3 in range(32):
                    inst = eng.matmul(
                        E["ps2"][0:128, 0:512],
                        E["WB"][:, j3 * 128:(j3 + 1) * 128], hslot(j3),
                        start=(j3 == 0), stop=(j3 == 31))
                return inst
            k.multi("t", mm_h1, r=("WB",) + htags, w=("ps2",))
            k.op("s", lambda e, _oc2=oc2: e.activation(
                E["SB2"][:, 0:512], E["ps2"][0:128, 0:512], AF.Gelu,
                scale=E["SCL"][:, 64 + _oc2:65 + _oc2]), r=("ps2", "SCL"), w=("SB2",))
            k.dma("g", io["hb"][oc2 * 128:(oc2 + 1) * 128, 0:512], E["SB2"][:, 0:512],
                  r=("SB2",), w=("HBMB",))
        for j4 in range(32):
            k.dma("q", hslot(j4), io["hb"][j4 * 128:(j4 + 1) * 128, 0:512],
                  r=("HBMB",), w=(f"H{j4}",))
        for oc2 in range(32):
            for j2 in range(8):
                k.dma("q" if j2 % 2 == 0 else "g",
                      E["WB"][:, j2 * 512:(j2 + 1) * 512].rearrange(
                          "p (a o) -> p a o", a=4),
                      io["qh2"][j2 * 512:(j2 + 1) * 512,
                                oc2 * 128:(oc2 + 1) * 128].rearrange(
                          "(a p) o -> p a o", p=128), r=("HBMQ",), w=("WB",))

            def mm_h2(eng):
                for j3 in range(32):
                    inst = eng.matmul(
                        E["ps2"][0:128, 0:512],
                        E["WB"][:, j3 * 128:(j3 + 1) * 128], hslot(j3),
                        start=(j3 == 0), stop=(j3 == 31))
                return inst
            k.multi("t", mm_h2, r=("WB",) + htags, w=("ps2",))
            k.op("s", lambda e, _oc2=oc2: e.activation(
                E["SB2"][:, 0:512], E["ps2"][0:128, 0:512], AF.Gelu,
                scale=E["SCL"][:, 96 + _oc2:97 + _oc2]), r=("ps2", "SCL"), w=("SB2",))
            k.dma("g", io["hb"][oc2 * 128:(oc2 + 1) * 128, 0:512], E["SB2"][:, 0:512],
                  r=("SB2",), w=("HBMB",))
        for j4 in range(32):
            k.dma("q", hslot(j4), io["hb"][j4 * 128:(j4 + 1) * 128, 0:512],
                  r=("HBMB",), w=(f"H{j4}",))
        for ocd in range(8):
            for j2 in range(8):
                k.dma("q" if j2 % 2 == 0 else "g",
                      E["WB"][:, j2 * 512:(j2 + 1) * 512].rearrange(
                          "p (a o) -> p a o", a=4),
                      io["q3"][j2 * 512:(j2 + 1) * 512,
                               ocd * 128:(ocd + 1) * 128].rearrange(
                          "(a p) o -> p a o", p=128), r=("HBMQ",), w=("WB",))

            def mm_3(eng):
                for j3 in range(32):
                    inst = eng.matmul(
                        E["ps2"][0:128, 0:512],
                        E["WB"][:, j3 * 128:(j3 + 1) * 128], hslot(j3),
                        start=(j3 == 0), stop=(j3 == 31))
                return inst
            k.multi("t", mm_3, r=("WB",) + htags, w=("ps2",))
            k.op("s", lambda e, _ocd=ocd: e.activation(
                E["SC"][:, 0:512], E["ps2"][0:128, 0:512], AF.Copy,
                scale=E["SCL"][:, 128 + _ocd:129 + _ocd]), r=("ps2", "SCL"), w=("SC",))
            k.op("v", lambda e, _ocd=ocd, _sl=sl: e.tensor_add(
                xr[_ocd][:, _sl], xr[_ocd][:, _sl], E["SC"][:, 0:512]),
                r=("SC", f"XR{ocd}"), w=(f"XR{ocd}",))


def build_l2_ffn_program():
    nc = bass.Bass("TRN2", target_bir_lowering=False, debug=False,
                   detect_race_conditions=False)
    x_in = nc.dram_tensor("x_in", [D, TC], F32, kind="ExternalInput").ap()
    kon = nc.dram_tensor("kon", [128, 140], F32, kind="ExternalInput").ap()
    names = {"w1T": [1024, 4096], "w2T": [1024, 4096], "h1T": [4096, 4096],
             "h2T": [4096, 4096], "w3T": [4096, 1024], "scl": [128, 136]}
    io = {}
    for n, shp in names.items():
        io[n] = nc.dram_tensor(n, shp, F32, kind="ExternalInput").ap()
    for n, shp in {"q1": [1024, 4096], "q2": [1024, 4096], "qh1": [4096, 4096],
                   "qh2": [4096, 4096], "q3": [4096, 1024],
                   "hb": [4096, 512]}.items():
        io[n] = nc.dram_tensor(n, shp, BF16, kind="Internal").ap()
    x_out = nc.dram_tensor("x_out", [D, TC], F32, kind="ExternalOutput").ap()
    es = ExitStack()
    k = K(nc)
    with es:
        k.start(es)
        E = alloc_env(k)
        emit_env_init(k, E, kon)
        emit_load_resid(k, E, x_in)
        emit_l2_ffn(k, E, io)
        emit_store_resid(k, E, x_out)
        k.finalize()
    return nc


def l2_ffn_host_inputs(inputs):
    scl = np.zeros((128, 136), np.float32)
    scl[:, 0:32] = inputs["L2_ffn_w1_s"].reshape(32, 128).T
    scl[:, 32:64] = inputs["L2_ffn_w2_s"].reshape(32, 128).T
    scl[:, 64:96] = inputs["L2_ffn_h1_s"].reshape(32, 128).T
    scl[:, 96:128] = inputs["L2_ffn_h2_s"].reshape(32, 128).T
    scl[:, 128:136] = inputs["L2_ffn_w3_s"].reshape(8, 128).T
    return {"w1T": _t(inputs["L2_ffn_w1_w"]), "w2T": _t(inputs["L2_ffn_w2_w"]),
            "h1T": _t(inputs["L2_ffn_h1_w"]), "h2T": _t(inputs["L2_ffn_h2_w"]),
            "w3T": _t(inputs["L2_ffn_w3_w"]), "scl": scl, "kon": konst_host()}


def run_l2_ffn(x_cores, inputs):
    nc = build_l2_ffn_program()
    hin = l2_ffn_host_inputs(inputs)
    in_maps = [dict(hin, x_in=x_cores[c]) for c in range(NCORE)]
    res = run_bass_kernel_spmd(nc, in_maps, core_ids=list(range(NCORE)))
    return [res.results[c]["x_out"] for c in range(NCORE)]


DIAG_SET = [list(range(4 * t, 4 * t + 4)) + list(range(16 + 4 * t, 16 + 4 * t + 4))
            for t in range(4)]


def emit_attention(k, E, io, cfg):
    xr = [E["XR"][:, c * TC:(c + 1) * TC] for c in range(NCH)]
    lnf = [E["LNB"][:, c * S:(c + 1) * S] for c in range(NCH)]
    lntags = tuple(f"LN{c}" for c in range(NCH))
    xrtags = tuple(f"XR{c}" for c in range(NCH))
    lora = cfg.get("lora", False)
    alibi = cfg.get("alibi")
    causal = cfg.get("causal", False)
    detour = causal or (alibi is not None)

    own = [lnf[c][:, 0:TC] for c in range(NCH)]
    prt = [lnf[c][:, TC:S] for c in range(NCH)]
    emit_ln(k, E, xr, own, TC, xrtags, lntags)
    for c in range(NCH):
        k.dma("q", E["WS"][:, 0:2048], io["x_partner"][c * 128:(c + 1) * 128, :],
              w=("WS",))
        k.op("v", lambda e, _c=c: e.tensor_copy(prt[_c], E["WS"][:, 0:2048]),
             r=("WS",), w=lntags)
    emit_ln(k, E, prt, prt, TC, lntags, lntags, src_bf16=True)

    k.dma("q", E["SCL"][:, 0:224], io["scl"], w=("SCL",))
    if detour:
        k.dma("q", E["BA"][:], io["cpat"], w=("BA",))
    if alibi is not None:
        k.dma("q", E["R4"][0:1, 0:TC], io["iota_own"], w=("R4",))
    if lora:
        for i, nm in enumerate(("aqT", "akT", "avT", "aoT")):
            for c in range(NCH):
                k.dma("q", E["WS"][:, 0:16], io[nm][c * 128:(c + 1) * 128, :],
                      w=("WS",))
                k.op("v", lambda e, _i=i, _c=c: e.tensor_copy(
                    E["LORA"][:, _i * 128 + _c * 16:_i * 128 + _c * 16 + 16],
                    E["WS"][:, 0:16]), r=("WS",), w=("LORA",))
    k.op("v", lambda e: e.memset(
        E["VT"][:].rearrange("p (s c) -> p s c", c=66)[:, :, 64:65], 1.0), w=("VT",))
    k.op("v", lambda e: e.memset(E["R5"][0:1, 0:512], 1.0), w=("R5",))
    for st8 in range(8):
        def mm_ka(eng, _s=st8):
            return eng.matmul(E["psB"][0:2, 0:512], E["IDB"][0:1, 0:2],
                              E["R5"][0:1, 0:512], start=True, stop=True)
        k.multi("t", mm_ka, r=("R5", "IDB"), w=("psB",))
        k.op("s", lambda e, _s=st8: e.copy(
            E["KA"][64:66, _s * 512:(_s + 1) * 512], E["psB"][0:2, 0:512]),
            r=("psB",), w=("KA",))

    qsc = E["SCL"][0:64, 200:201]

    for h in range(16):
        qb = E["SCL"][0:64, 136 + h:137 + h] if cfg.get("qkv_bias") else None
        kb = E["SCL"][0:64, 152 + h:153 + h] if cfg.get("qkv_bias") else None
        vb = E["SCL"][0:64, 168 + h:169 + h] if cfg.get("qkv_bias") else None
        for si, wkey in enumerate((cfg["wq"], cfg["wk"], cfg["wv"])):
            for c in range(NCH):
                k.dma("q", E["WS"][:, c * 64:(c + 1) * 64],
                      io[wkey][c * 128:(c + 1) * 128, h * 64:(h + 1) * 64],
                      w=("WS",))
            k.op("v", lambda e, _si=si: e.tensor_copy(
                E["WB"][:, _si * 512:(_si + 1) * 512], E["WS"][:, 0:512]),
                r=("WS",), w=("WB",))
        if lora:
            for si in range(4):
                nm = ("bqT", "bkT", "bvT", "boT")[si]
                cols = 128 if si == 3 else 64
                if si < 3:
                    k.dma("g", E["WS"][0:16, 0:64],
                          io[nm][0:16, h * 64:(h + 1) * 64], w=("WS",))
                    k.op("v", lambda e, _si=si: e.tensor_copy(
                        E["WB"][0:16, 1536 + _si * 64:1536 + _si * 64 + 64],
                        E["WS"][0:16, 0:64]), r=("WS",), w=("WB",))

        def proj(dst_ap, dst_tag, wslot, bias_col, n_tiles, lora_side, scale_ap,
                 ps):
            for tt in range(n_tiles):
                c0 = tt * 512
                if lora_side is not None:
                    def mm_mid(eng, _c0=c0, _li=lora_side):
                        for c in range(NCH):
                            inst = eng.matmul(
                                E["ps4"][0:16, 0:512],
                                E["LORA"][:, _li * 128 + c * 16:_li * 128 + c * 16 + 16],
                                lnf[c][:, _c0:_c0 + 512], start=(c == 0),
                                stop=(c == 7))
                        return inst
                    k.multi("t", mm_mid, r=("LORA",) + lntags, w=("ps4",))
                    k.op("s", lambda e: e.activation(
                        E["SB2"][0:16, 512:1024], E["ps4"][0:16, 0:512], AF.Copy,
                        scale=kc(E, 4)[0:16, :]), r=("ps4", "KONF"), w=("SB2",))

                def mm_main(eng, _c0=c0, _ws=wslot, _li=lora_side, _ps=ps):
                    for c in range(NCH):
                        inst = eng.matmul(
                            E[_ps][0:64, 0:512],
                            E["WB"][:, _ws * 512 + c * 64:_ws * 512 + c * 64 + 64],
                            lnf[c][:, _c0:_c0 + 512], start=(c == 0),
                            stop=(c == 7 and _li is None))
                    if _li is not None:
                        inst = eng.matmul(
                            E[_ps][0:64, 0:512],
                            E["WB"][0:16, 1536 + _li * 64:1536 + _li * 64 + 64],
                            E["SB2"][0:16, 512:1024], start=False, stop=True)
                    return inst
                k.multi("t", mm_main, r=("WB", "SB2") + lntags, w=(ps,))
                if bias_col is not None and scale_ap is not None:
                    k.op("v", lambda e, _tt=tt, _ps=ps, _b=bias_col, _s=scale_ap:
                         e.tensor_scalar(dst_ap[0:64, _tt * 512:(_tt + 1) * 512],
                                         E[_ps][0:64, 0:512], _s, _b,
                                         ALU.mult, ALU.add),
                         r=(ps, "SCL"), w=(dst_tag,))
                elif bias_col is not None:
                    k.op("v", lambda e, _tt=tt, _ps=ps, _b=bias_col:
                         e.tensor_scalar(dst_ap[0:64, _tt * 512:(_tt + 1) * 512],
                                         E[_ps][0:64, 0:512], _b, None, ALU.add),
                         r=(ps, "SCL"), w=(dst_tag,))
                elif scale_ap is not None:
                    k.op("s", lambda e, _tt=tt, _ps=ps, _s=scale_ap: e.activation(
                        dst_ap[0:64, _tt * 512:(_tt + 1) * 512], E[_ps][0:64, 0:512],
                        AF.Copy, scale=_s), r=(ps, "SCL"), w=(dst_tag,))
                else:
                    k.op("s", lambda e, _tt=tt, _ps=ps: e.copy(
                        dst_ap[0:64, _tt * 512:(_tt + 1) * 512], E[_ps][0:64, 0:512]),
                        r=(ps,), w=(dst_tag,))

        proj(E["KA"], "KA", 1, kb, 8, (1 if lora else None), None, "ps0")
        for vt in range(8):
            if lora:
                def mm_vmid(eng, _vt=vt):
                    for c in range(NCH):
                        inst = eng.matmul(
                            E["ps4"][0:16, 0:512],
                            E["LORA"][:, 2 * 128 + c * 16:2 * 128 + c * 16 + 16],
                            lnf[c][:, _vt * 512:_vt * 512 + 512], start=(c == 0),
                            stop=(c == 7))
                    return inst
                k.multi("t", mm_vmid, r=("LORA",) + lntags, w=("ps4",))
                k.op("s", lambda e: e.activation(
                    E["SB2"][0:16, 512:1024], E["ps4"][0:16, 0:512], AF.Copy,
                    scale=kc(E, 4)[0:16, :]), r=("ps4", "KONF"), w=("SB2",))

            def mm_v(eng, _vt=vt, _h=h):
                for c in range(NCH):
                    inst = eng.matmul(
                        E["ps1"][0:64, 0:512],
                        E["WB"][:, 1024 + c * 64:1024 + c * 64 + 64],
                        lnf[c][:, _vt * 512:_vt * 512 + 512], start=(c == 0),
                        stop=(c == 7 and not lora))
                if lora:
                    inst = eng.matmul(
                        E["ps1"][0:64, 0:512],
                        E["WB"][0:16, 1536 + 2 * 64:1536 + 2 * 64 + 64],
                        E["SB2"][0:16, 512:1024], start=False, stop=True)
                return inst
            k.multi("t", mm_v, r=("WB", "SB2") + lntags, w=("ps1",))
            if vb is not None:
                k.op("v", lambda e, _b=vb: e.tensor_scalar(
                    E["SB2"][0:64, 0:512], E["ps1"][0:64, 0:512], _b, None,
                    ALU.add), r=("ps1", "SCL"), w=("SB2",))
            else:
                k.op("s", lambda e: e.copy(E["SB2"][0:64, 0:512],
                                           E["ps1"][0:64, 0:512]),
                     r=("ps1",), w=("SB2",))
            for sub in range(4):
                slab = vt * 4 + sub
                k.multi("t", lambda e, _s=sub: e.transpose(
                    E["psT"][0:128, 0:64], E["SB2"][0:64, _s * 128:(_s + 1) * 128],
                    E["IDB"][0:64, 0:64]), r=("SB2", "IDB"), w=("psT",))
                k.op("s", lambda e, _sl=slab: e.copy(
                    E["VT"][:, _sl * 66:_sl * 66 + 64], E["psT"][0:128, 0:64]),
                    r=("psT",), w=("VT",))
        proj(E["QA"], "QA", 0, qb, 4, (0 if lora else None), qsc, "ps0")
        for vt in range(8):
            k.op("s", lambda e, _vt=vt: e.activation(
                E["SC"][0:64, 0:512], E["KA"][0:64, _vt * 512:_vt * 512 + 512],
                AF.Square), r=("KA",), w=("SC",))
            k.multi("t", lambda e: e.matmul(
                E["psB"][0:1, 0:512], kc(E, 0)[0:64, :], E["SC"][0:64, 0:512],
                start=True, stop=True), r=("SC", "KONF"), w=("psB",))
            k.op("v", lambda e, _vt=vt: e.reduce_max(
                E["CT"][0:1, 20:21], E["psB"][0:1, 0:512], axis=AX.X),
                r=("psB",), w=("CT",))
            if vt == 0:
                k.op("v", lambda e: e.tensor_copy(
                    E["CT"][0:1, 19:20], E["CT"][0:1, 20:21]), r=("CT",), w=("CT",))
            else:
                k.op("v", lambda e: e.tensor_max(
                    E["CT"][0:1, 19:20], E["CT"][0:1, 19:20], E["CT"][0:1, 20:21]),
                    r=("CT",), w=("CT",))
        k.op("s", lambda e: e.activation(
            E["CT"][0:1, 19:20], E["CT"][0:1, 19:20], AF.Sqrt), r=("CT",), w=("CT",))
        for tt in range(4):
            k.op("s", lambda e, _tt=tt: e.activation(
                E["SC"][0:64, 0:512], E["QA"][0:64, _tt * 512:_tt * 512 + 512],
                AF.Square), r=("QA",), w=("SC",))
            k.multi("t", lambda e: e.matmul(
                E["psB"][0:1, 0:512], kc(E, 0)[0:64, :], E["SC"][0:64, 0:512],
                start=True, stop=True), r=("SC", "KONF"), w=("psB",))
            k.op("s", lambda e, _tt=tt: e.activation(
                E["R1"][0:1, _tt * 512:_tt * 512 + 512], E["psB"][0:1, 0:512],
                AF.Sqrt), r=("psB",), w=("R1",))
        k.op("v", lambda e: e.scalar_tensor_tensor(
            E["R3"][0:1, 0:TC], E["R1"][0:1, 0:TC], -1.0, E["R1"][0:1, 0:TC],
            ALU.mult, ALU.bypass), r=("R1",), w=("R3",))
        k.op("v", lambda e: e.tensor_scalar(
            E["R3"][0:1, 0:TC], E["R3"][0:1, 0:TC], E["CT"][0:1, 19:20], None,
            ALU.mult), r=("R3", "CT"), w=("R3",))
        if alibi is not None:
            k.op("v", lambda e, _sl2=float(alibi[h]): e.tensor_scalar(
                E["BAH"][:], E["BA"][:], _sl2, None, ALU.mult), r=("BA",), w=("BAH",))
            k.op("v", lambda e, _sl2=float(alibi[h]): e.tensor_scalar(
                E["R2"][0:1, 0:TC], E["R4"][0:1, 0:TC], -_sl2, None, ALU.mult),
                r=("R4",), w=("R2",))
        for tt in range(4):
            def mm_qa(eng, _tt=tt, _al=(alibi is not None)):
                inst = eng.matmul(
                    E["psB"][0:2, 0:512], E["IDB"][0:1, 0:2],
                    E["R3"][0:1, _tt * 512:(_tt + 1) * 512], start=True,
                    stop=not _al)
                if _al:
                    inst = eng.matmul(
                        E["psB"][0:2, 0:512], E["SELB"][0:1, 0:2],
                        E["R2"][0:1, _tt * 512:(_tt + 1) * 512], start=False,
                        stop=True)
                return inst
            k.multi("t", mm_qa, r=("IDB", "SELB", "R3", "R2"), w=("psB",))
            k.op("s", lambda e, _tt=tt: e.copy(
                E["QA"][64:66, _tt * 512:(_tt + 1) * 512], E["psB"][0:2, 0:512]),
                r=("psB",), w=("QA",))
        for tqt in range(4):
            tq0 = tqt * 512
            if alibi is not None:
                for st8 in range(8):
                    k.dma("q", E["SC"][0:1, 1024:1536],
                          io["sgn"][tqt:tqt + 1, st8 * 512:(st8 + 1) * 512],
                          w=("SC",))
                    k.op("v", lambda e: e.tensor_copy(
                        E["R5"][0:1, 512:1024], E["SC"][0:1, 1024:1536]),
                        r=("SC",), w=("R5",))

                    def mm_ka2(eng, _s=st8):
                        inst = eng.matmul(E["psB"][0:2, 0:512], E["IDB"][0:1, 0:2],
                                          E["R5"][0:1, 0:512], start=True, stop=False)
                        inst = eng.matmul(E["psB"][0:2, 0:512], E["SELB"][0:1, 0:2],
                                          E["R5"][0:1, 512:1024], start=False,
                                          stop=True)
                        return inst
                    k.multi("t", mm_ka2, r=("R5", "IDB", "SELB"), w=("psB",))
                    k.op("s", lambda e, _s=st8: e.copy(
                        E["KA"][64:66, _s * 512:(_s + 1) * 512], E["psB"][0:2, 0:512]),
                        r=("psB",), w=("KA",))
            for tkb in range(32):
                def mm_s(eng, _tkb=tkb, _tq0=tq0):
                    return eng.matmul(
                        E["ps2"][0:128, 0:512],
                        E["KA"][0:66, _tkb * 128:(_tkb + 1) * 128],
                        E["QA"][0:66, _tq0:_tq0 + 512], start=True, stop=True)
                k.multi("t", mm_s, r=("KA", "QA"), w=("ps2",))
                is_det = detour and (tkb in DIAG_SET[tqt])
                bia = None
                if causal:
                    bia = ("BA", E["BA"][:, tqt * 32 + tkb:tqt * 32 + tkb + 1])
                elif alibi is not None:
                    bia = ("BAH", E["BAH"][:, tqt * 32 + tkb:tqt * 32 + tkb + 1])
                if is_det:
                    di = DIAG_SET[tqt].index(tkb)
                    slot = di % 2
                    k.dma("q", E["WS"][:, slot * 512:slot * 512 + 512],
                          io["mdiag"][(tqt * 8 + di) * 128:(tqt * 8 + di + 1) * 128, :],
                          w=("WS",))
                    mul = (-float(alibi[h])) if alibi is not None else 1.0
                    k.op("v", lambda e, _sv=slot, _m=mul: e.scalar_tensor_tensor(
                        E["SC"][:, 0:512], E["WS"][:, _sv * 512:_sv * 512 + 512],
                        _m, E["ps2"][0:128, 0:512], ALU.mult, ALU.add),
                        r=("WS", "ps2"), w=("SC",))
                    src_ap, src_tag = E["SC"][:, 0:512], "SC"
                else:
                    src_ap, src_tag = E["ps2"][0:128, 0:512], "ps2"
                if bia is not None:
                    k.op("s", lambda e, _a=src_ap, _b=bia: e.activation(
                        E["SB2"][0:128, 1024:1536], _a, AF.Exp, bias=_b[1]),
                        r=(src_tag, bia[0]), w=("SB2",))
                else:
                    k.op("s", lambda e, _a=src_ap: e.activation(
                        E["SB2"][0:128, 1024:1536], _a, AF.Exp),
                        r=(src_tag,), w=("SB2",))

                def mm_pv(eng, _tkb=tkb):
                    return eng.matmul(
                        E["ps3"][0:65, 0:512], E["VT"][:, _tkb * 66:_tkb * 66 + 65],
                        E["SB2"][0:128, 1024:1536], start=(_tkb == 0),
                        stop=(_tkb == 31))
                k.multi("t", mm_pv, r=("VT", "SB2"), w=("ps3",))
            k.op("v", lambda e: e.tensor_copy(
                E["R1"][0:1, 0:512], E["ps3"][64:65, 0:512]), r=("ps3",), w=("R1",))
            k.op("v", lambda e: e.reciprocal(
                E["R1"][0:1, 512:1024], E["R1"][0:1, 0:512]), r=("R1",), w=("R1",))
            k.multi("t", lambda e: e.matmul(
                E["psB"][0:128, 0:512], E["ONE1F"][0:1, 0:128],
                E["R1"][0:1, 512:1024], start=True, stop=True),
                r=("R1", "ONE1F"), w=("psB",))
            k.op("s", lambda e: e.copy(E["SC"][0:64, 0:512], E["ps3"][0:64, 0:512]),
                 r=("ps3",), w=("SC",))
            k.op("v", lambda e: e.tensor_mul(
                E["SB2"][0:64, 1024:1536], E["SC"][0:64, 0:512],
                E["psB"][0:64, 0:512]), r=("SC", "psB"), w=("SB2",))
            k.dma("g", io["ctxh"][h * 64:(h + 1) * 64, tq0:tq0 + 512],
                  E["SB2"][0:64, 1024:1536], r=("SB2",), w=("HBMC",))
    for tqt in range(4):
        tq0 = tqt * 512
        for c in range(NCH):
            k.dma("q", E["KA"][:, c * 512:(c + 1) * 512],
                  io["ctxh"][c * 128:(c + 1) * 128, tq0:tq0 + 512],
                  r=("HBMC",), w=("KA",))
        if lora:
            def mm_omid(eng):
                for c in range(NCH):
                    inst = eng.matmul(
                        E["ps4"][0:16, 0:512],
                        E["LORA"][:, 3 * 128 + c * 16:3 * 128 + c * 16 + 16],
                        E["KA"][:, c * 512:(c + 1) * 512],
                        start=(c == 0), stop=(c == 7))
                return inst
            k.multi("t", mm_omid, r=("LORA", "KA"), w=("ps4",))
            k.op("s", lambda e: e.activation(
                E["SB2"][0:16, 512:1024], E["ps4"][0:16, 0:512], AF.Copy,
                scale=kc(E, 4)[0:16, :]), r=("ps4", "KONF"), w=("SB2",))
        for och in range(NCH):
            for c in range(NCH):
                k.dma("g", E["WS"][:, (c % 4) * 128:(c % 4) * 128 + 128],
                      io["woT"][c * 128:(c + 1) * 128, och * 128:(och + 1) * 128],
                      w=("WS",))
                k.op("v", lambda e, _c=c: e.tensor_copy(
                    E["WB"][:, _c * 128:_c * 128 + 128],
                    E["WS"][:, (_c % 4) * 128:(_c % 4) * 128 + 128]),
                    r=("WS",), w=("WB",))
            if lora:
                k.dma("g", E["WS"][0:16, 512:640],
                      io["boT"][0:16, och * 128:(och + 1) * 128], w=("WS",))
                k.op("v", lambda e: e.tensor_copy(
                    E["WB"][0:16, 1536:1664], E["WS"][0:16, 512:640]),
                    r=("WS",), w=("WB",))

            def mm_o(eng, _lora=lora):
                for c in range(NCH):
                    inst = eng.matmul(
                        E["ps0"][0:128, 0:512], E["WB"][:, c * 128:c * 128 + 128],
                        E["KA"][:, c * 512:(c + 1) * 512],
                        start=(c == 0), stop=(c == 7 and not _lora))
                if _lora:
                    inst = eng.matmul(
                        E["ps0"][0:128, 0:512], E["WB"][0:16, 1536:1664],
                        E["SB2"][0:16, 512:1024], start=False, stop=True)
                return inst
            k.multi("t", mm_o, r=("WB", "KA", "SB2"), w=("ps0",))
            if cfg.get("o_bias"):
                k.op("v", lambda e, _o=och: e.tensor_scalar(
                    E["SC"][:, 0:512], E["ps0"][0:128, 0:512],
                    E["SCL"][:, 184 + _o:185 + _o], None, ALU.add),
                    r=("ps0", "SCL"), w=("SC",))
            else:
                k.op("s", lambda e: e.copy(E["SC"][:, 0:512], E["ps0"][0:128, 0:512]),
                     r=("ps0",), w=("SC",))
            k.op("v", lambda e, _o=och, _t0=tq0: e.tensor_add(
                xr[_o][:, _t0:_t0 + 512], xr[_o][:, _t0:_t0 + 512],
                E["SC"][:, 0:512]), r=("SC", f"XR{och}"), w=(f"XR{och}",))


ATTN_IO_SHAPES = {
    "wqT": [1024, 1024], "wkT": [1024, 1024], "wvT": [1024, 1024],
    "woT": [1024, 1024], "scl": [128, 224],
}


def build_attn_program(cfg, with_lora=False, with_detour=False, with_iota=False):
    nc = bass.Bass("TRN2", target_bir_lowering=False, debug=False,
                   detect_race_conditions=False)
    x_in = nc.dram_tensor("x_in", [D, TC], F32, kind="ExternalInput").ap()
    x_prt = nc.dram_tensor("x_prt", [D, TC], F32, kind="ExternalInput").ap()
    kon = nc.dram_tensor("kon", [128, 140], F32, kind="ExternalInput").ap()
    io = {"x_partner": x_prt}
    for n, shp in ATTN_IO_SHAPES.items():
        io[n] = nc.dram_tensor(n, shp, F32, kind="ExternalInput").ap()
    if with_lora:
        for n in ("aqT", "akT", "avT", "aoT"):
            io[n] = nc.dram_tensor(n, [1024, 16], F32, kind="ExternalInput").ap()
        for n in ("bqT", "bkT", "bvT", "boT"):
            io[n] = nc.dram_tensor(n, [16, 1024], F32, kind="ExternalInput").ap()
    if with_detour:
        io["cpat"] = nc.dram_tensor("cpat", [128, 128], F32, kind="ExternalInput").ap()
        io["mdiag"] = nc.dram_tensor("mdiag", [4096, 512], F32, kind="ExternalInput").ap()
    if with_iota:
        io["iota_own"] = nc.dram_tensor("iota_own", [1, TC], F32, kind="ExternalInput").ap()
        io["sgn"] = nc.dram_tensor("sgn", [4, S], F32, kind="ExternalInput").ap()
    io["ctxh"] = nc.dram_tensor("ctxh", [1024, 2048], BF16, kind="Internal").ap()
    x_out = nc.dram_tensor("x_out", [D, TC], F32, kind="ExternalOutput").ap()
    es = ExitStack()
    k = K(nc)
    with es:
        k.start(es)
        E = alloc_env(k)
        emit_env_init(k, E, kon)
        emit_load_resid(k, E, x_in)
        emit_attention(k, E, io, cfg)
        emit_store_resid(k, E, x_out)
        k.finalize()
    return nc


def attn_host_scl(cfg, inputs, pre):
    scl = np.zeros((128, 224), np.float32)
    if cfg.get("qkv_bias"):
        qb = inputs[pre + cfg["qb"]].reshape(16, 64).T
        kb = inputs[pre + cfg["kb"]].reshape(16, 64).T
        vb = inputs[pre + cfg["vb"]].reshape(16, 64).T
        scl[0:64, 136:152] = qb
        scl[0:64, 152:168] = kb
        scl[0:64, 168:184] = vb
    if cfg.get("o_bias"):
        scl[:, 184:192] = inputs[pre + cfg["ob"]].reshape(8, 128).T
    scl[:, 200] = cfg["scale"]
    return scl


def make_cpat_mdiag(half, causal, alibi):
    cpat = np.zeros((128, 128), np.float32)
    mdiag = np.zeros((4, 8, 128, 512), np.float32)
    for tqt in range(4):
        tq_g0 = half * 2048 + tqt * 512
        for tkb in range(32):
            p = np.arange(128)
            jl0 = tkb * 128
            if jl0 < 2048:
                j_g = half * 2048 + jl0 + p
            else:
                j_g = (1 - half) * 2048 + (jl0 - 2048) + p
            col = tqt * 32 + tkb
            in_det = tkb in DIAG_SET[tqt]
            crosses = (j_g[0] <= tq_g0 + 511) and (j_g[-1] >= tq_g0)
            if causal:
                if crosses and in_det:
                    di = DIAG_SET[tqt].index(tkb)
                    f = np.arange(512)
                    mdiag[tqt, di] = np.where(
                        j_g[:, None] > (tq_g0 + f)[None, :], -1e30, 0.0)
                    cpat[:, col] = 0.0
                elif crosses:
                    raise RuntimeError("diag outside detour set")
                else:
                    blocked = j_g[0] > tq_g0 + 511
                    cpat[:, col] = -1e30 if blocked else 0.0
            elif alibi is not None:
                if crosses and in_det:
                    di = DIAG_SET[tqt].index(tkb)
                    f = np.arange(512)
                    mdiag[tqt, di] = np.abs(j_g[:, None] - (tq_g0 + f)[None, :])
                    cpat[:, col] = 0.0
                elif crosses:
                    raise RuntimeError("diag outside detour set")
                else:
                    cpat[:, col] = np.where(j_g < tq_g0, j_g, -j_g)
    return cpat, np.ascontiguousarray(mdiag.reshape(4096, 512))


def attn_cfg(layer, inputs):
    if layer == "L1":
        cfg = {"scale": 1.0 / 32.0, "qkv_bias": True, "o_bias": True,
               "wq": "wkT", "wk": "wqT", "wv": "wvT"}
        scl = np.zeros((128, 224), np.float32)
        scl[0:64, 136:152] = inputs["L1_attn_bk"].reshape(16, 64).T
        scl[0:64, 152:168] = inputs["L1_attn_bq"].reshape(16, 64).T
        scl[0:64, 168:184] = inputs["L1_attn_bv"].reshape(16, 64).T
        scl[:, 184:192] = inputs["L1_attn_bo"].reshape(8, 128).T
        scl[:, 200] = cfg["scale"]
        host = {"wqT": _t(inputs["L1_attn_Wq"]), "wkT": _t(inputs["L1_attn_Wk"]),
                "wvT": _t(inputs["L1_attn_Wv"]), "woT": _t(inputs["L1_attn_Wo"]),
                "scl": scl, "kon": konst_host()}
        return cfg, host, (False, False, False)
    if layer == "L0":
        cfg = {"scale": 1.0 / 8.0, "causal": True, "lora": True,
               "wq": "wqT", "wk": "wkT", "wv": "wvT"}
        scl = np.zeros((128, 224), np.float32)
        scl[:, 200] = cfg["scale"]
        host = {"wqT": _t(inputs["L0_attn_wq"]), "wkT": _t(inputs["L0_attn_wk"]),
                "wvT": _t(inputs["L0_attn_wv"]), "woT": _t(inputs["L0_attn_wo"]),
                "aqT": _t(inputs["L0_attn_aq"]), "akT": _t(inputs["L0_attn_ak"]),
                "avT": _t(inputs["L0_attn_av"]), "aoT": _t(inputs["L0_attn_ao"]),
                "bqT": _t(inputs["L0_attn_bq"]), "bkT": _t(inputs["L0_attn_bk"]),
                "bvT": _t(inputs["L0_attn_bv"]), "boT": _t(inputs["L0_attn_bo"]),
                "scl": scl, "kon": konst_host()}
        return cfg, host, (True, True, False)
    if layer == "L2":
        n = 16
        start = 2.0 ** (-2.0 ** (-(np.log2(n) - 3)))
        slopes = [float(start * start ** i) for i in range(n)]
        cfg = {"scale": 1.0 / 8.0, "alibi": slopes, "o_bias": True,
               "wq": "wqT", "wk": "wkT", "wv": "wvT"}
        scl = np.zeros((128, 224), np.float32)
        scl[:, 184:192] = inputs["L2_attn_bo"].reshape(8, 128).T
        scl[:, 200] = cfg["scale"]
        host = {"wqT": _t(inputs["L2_attn_Wq"]), "wkT": _t(inputs["L2_attn_Wk"]),
                "wvT": _t(inputs["L2_attn_Wv"]), "woT": _t(inputs["L2_attn_Wo"]),
                "scl": scl, "kon": konst_host()}
        return cfg, host, (False, True, True)
    raise ValueError(layer)


def run_attn(layer, x_cores, inputs):
    cfg, host, (wl, wd, wi) = attn_cfg(layer, inputs)
    nc = build_attn_program(cfg, with_lora=wl, with_detour=wd, with_iota=wi)
    in_maps = []
    for c in range(NCORE):
        m = dict(host)
        m["x_in"] = x_cores[c]
        m["x_prt"] = x_cores[c ^ 1]
        half = c % 2
        if wd:
            cpat, mdiag = make_cpat_mdiag(half, cfg.get("causal", False),
                                          cfg.get("alibi"))
            m["cpat"] = cpat
            m["mdiag"] = mdiag
        if wi:
            m["iota_own"] = (half * 2048 + np.arange(2048, dtype=np.float32))[None, :]
            sgn = np.zeros((4, S), np.float32)
            for tqt in range(4):
                tq_lo = half * 2048 + tqt * 512
                tq_hi = tq_lo + 511
                for tkb in range(32):
                    jl0 = tkb * 128
                    jg0 = (half * 2048 + jl0 if jl0 < 2048
                           else (1 - half) * 2048 + jl0 - 2048)
                    if jg0 + 127 < tq_lo:
                        sgn[tqt, jl0:jl0 + 128] = 1.0
                    elif jg0 > tq_hi:
                        sgn[tqt, jl0:jl0 + 128] = -1.0
                    else:
                        sgn[tqt, jl0:jl0 + 128] = 0.0
            m["sgn"] = sgn
        in_maps.append(m)
    res = run_bass_kernel_spmd(nc, in_maps, core_ids=list(range(NCORE)))
    return [res.results[c]["x_out"] for c in range(NCORE)]


def emit_l1_ffn(k, E, io):
    xr = [E["XR"][:, c * TC:(c + 1) * TC] for c in range(NCH)]
    ln = [E["LNB"][:, c * S:c * S + TC] for c in range(NCH)]
    lntags = tuple(f"LN{c}" for c in range(NCH))
    xrtags = tuple(f"XR{c}" for c in range(NCH))
    MW = E["AR"][:, 2048:2176]

    def hsub(i):
        base = (i // 2) * S + TC + (i % 2) * 512
        return E["LNB"][:, base:base + 512]
    hsubtags = tuple(f"HS{i}" for i in range(16))

    emit_ln(k, E, xr, ln, TC, xrtags, lntags, keep_stats=True)
    k.dma("q", E["SCL"][:, 0:224], io["scl"], w=("SCL",))
    k.dma("q", E["BA"][:], io["rb1c"], w=("BA",))
    k.dma("q", E["BAH"][:], io["rb2c"], w=("BAH",))
    emit_l1_ffn_pre(k, E, io)
    def mm_cs(eng):
        for c in range(NCH):
            inst = eng.matmul(E["psB"][0:1, 0:8], kc(E, 0),
                              io_rw_tile(E, c), start=(c == 0), stop=(c == 7))
        return inst
    k.multi("t", mm_cs, r=("RWT", "KONF"), w=("psB",))
    k.op("v", lambda e: e.tensor_copy(E["SC"][0:1, 32:40], E["psB"][0:1, 0:8]),
         r=("psB",), w=("SC",))
    k.multi("t", lambda e: e.matmul(E["psB"][0:128, 0:8], E["ONE1F"][0:1, :],
                                    E["SC"][0:1, 32:40], start=True, stop=True),
            r=("SC", "ONE1F"), w=("psB",))
    k.op("s", lambda e: e.copy(E["BC"][:, 0:8], E["psB"][0:128, 0:8]),
         r=("psB",), w=("BC",))
    for tt in range(16):
        def mm_r(eng, _tt=tt):
            for c in range(NCH):
                inst = eng.matmul(
                    E["ps4"][0:128, 0:8], xr[c][:, _tt * 128:(_tt + 1) * 128],
                    io_rw_tile(E, c), start=(c == 0), stop=(c == 7))
            return inst
        k.multi("t", mm_r, r=("RWT",) + xrtags, w=("ps4",))
        k.multi("t", lambda e, _tt=tt: e.transpose(
            E["ps5"][0:128, 0:1], E["R1"][0:1, _tt * 128:(_tt + 1) * 128],
            E["IDF"][0:1, 0:1]), r=("R1", "IDF"), w=("ps5",))
        k.op("v", lambda e: e.tensor_scalar(
            E["SC"][:, 8:9], E["ps5"][0:128, 0:1], -1.0, None, ALU.mult),
            r=("ps5",), w=("SC",))
        k.multi("t", lambda e, _tt=tt: e.transpose(
            E["ps5"][0:128, 0:1], E["R4"][0:1, _tt * 128:(_tt + 1) * 128],
            E["IDF"][0:1, 0:1]), r=("R4", "IDF"), w=("ps5",))
        k.op("v", lambda e: e.tensor_copy(E["SC"][:, 9:10], E["ps5"][0:128, 0:1]),
             r=("ps5",), w=("SC",))
        k.op("v", lambda e: e.scalar_tensor_tensor(
            E["SC"][:, 0:8], E["BC"][:, 0:8], E["SC"][:, 8:9],
            E["ps4"][0:128, 0:8], ALU.mult, ALU.add), r=("BC", "SC", "ps4"), w=("SC",))
        k.op("v", lambda e: e.tensor_scalar(
            E["SC"][:, 0:8], E["SC"][:, 0:8], E["SC"][:, 9:10], None, ALU.mult),
            r=("SC",), w=("SC",))
        k.op("v", lambda e: e.tensor_add(
            E["SC"][:, 0:8], E["SC"][:, 0:8], E["SCL"][:, 208:216]),
            r=("SC", "SCL"), w=("SC",))
        k.op("v", lambda e: e.reduce_max(
            E["SC"][:, 8:9], E["SC"][:, 0:8], axis=AX.X, negate=True),
            r=("SC",), w=("SC",))
        k.op("s", lambda e: e.activation(
            E["SC"][:, 16:24], E["SC"][:, 0:8], AF.Exp, bias=E["SC"][:, 8:9]),
            r=("SC",), w=("SC",))
        k.op("v", lambda e: e.reduce_sum(
            E["SC"][:, 8:9], E["SC"][:, 16:24], axis=AX.X), r=("SC",), w=("SC",))
        k.op("v", lambda e: e.reciprocal(
            E["SC"][:, 9:10], E["SC"][:, 8:9]), r=("SC",), w=("SC",))
        k.op("v", lambda e: e.tensor_scalar(
            E["SC"][:, 16:24], E["SC"][:, 16:24], E["SC"][:, 9:10], None,
            ALU.mult), r=("SC",), w=("SC",))
        k.op("v", lambda e: e.reduce_max(
            E["SC"][:, 8:9], E["SC"][:, 16:24], axis=AX.X), r=("SC",), w=("SC",))
        k.op("v", lambda e: e.tensor_scalar(
            E["SC"][:, 24:32], E["SC"][:, 16:24], E["SC"][:, 8:9], None,
            ALU.is_ge), r=("SC",), w=("SC",))
        k.op("v", lambda e: e.scalar_tensor_tensor(
            E["SC"][:, 24:32], E["SC"][:, 24:32], -1e30, E["SC"][:, 16:24],
            ALU.mult, ALU.add), r=("SC",), w=("SC",))
        k.op("v", lambda e: e.reduce_max(
            E["SC"][:, 8:9], E["SC"][:, 24:32], axis=AX.X), r=("SC",), w=("SC",))
        k.op("v", lambda e: e.tensor_scalar(
            E["SC"][:, 24:32], E["SC"][:, 16:24], E["SC"][:, 8:9], None,
            ALU.is_ge), r=("SC",), w=("SC",))
        k.op("v", lambda e, _tt=tt: e.tensor_mul(
            MW[:, _tt * 8:(_tt + 1) * 8], E["SC"][:, 16:24], E["SC"][:, 24:32]),
            r=("SC",), w=("MW",))

    for ei in range(-1, 8):
        shared = (ei == -1)
        if not shared:
            for tt in range(16):
                k.multi("t", lambda e, _tt=tt, _e=ei: e.transpose(
                    E["psT"][0:1, 0:128], MW[:, _tt * 8 + _e:_tt * 8 + _e + 1],
                    E["IDB"][:]), r=("MW", "IDB"), w=("psT",))
                k.op("v", lambda e, _tt=tt: e.tensor_copy(
                    E["R5"][0:1, _tt * 128:(_tt + 1) * 128], E["psT"][0:1, 0:128]),
                    r=("psT",), w=("R5",))
        for hc in range(16):
            b1col = (E["BAH"][:, 64 + hc:65 + hc] if shared
                     else E["BA"][:, ei * 16 + hc:ei * 16 + hc + 1])
            if shared:
                src_ap = io["sw1"][:, hc * 128:(hc + 1) * 128]
            else:
                src_ap = io["rw1"][ei][:, hc * 128:(hc + 1) * 128]
            k.dma("q", E["WS"][:, 0:1024].rearrange("p (a o) -> p a o", a=8),
                  src_ap.rearrange("(a p) o -> p a o", p=128), w=("WS",))
            k.op("v", lambda e: e.tensor_copy(
                E["WB"][:, 0:1024], E["WS"][:, 0:1024]), r=("WS",), w=("WB",))
            for tt in range(4):
                def mm_h1(eng, _tt=tt):
                    for kcc in range(8):
                        inst = eng.matmul(
                            E[f"ps{_tt}"][0:128, 0:512],
                            E["WB"][:, kcc * 128:(kcc + 1) * 128],
                            ln[kcc][:, _tt * 512:(_tt + 1) * 512],
                            start=(kcc == 0), stop=(kcc == 7))
                    return inst
                k.multi("t", mm_h1, r=("WB",) + lntags, w=(f"ps{tt}",))
            for tt in range(4):
                k.op("s", lambda e, _tt=tt, _b=b1col: e.activation(
                    E["SB2"][:, 0:512], E[f"ps{_tt}"][0:128, 0:512], AF.Gelu,
                    bias=_b), r=(f"ps{tt}", "BA", "BAH"), w=("SB2",))
                k.dma("g", io["h1s"][hc * 128:(hc + 1) * 128,
                                     tt * 512:(tt + 1) * 512],
                      E["SB2"][:, 0:512], r=("SB2",), w=("HBMH",))
        for tt in range(4):
            for b2 in range(2):
                dst = E["LNB"][:].rearrange("p (c q) -> p c q", q=S)[
                    :, :, TC + b2 * 512:TC + (b2 + 1) * 512]
                srcb = io["h1s"][:, tt * 512:(tt + 1) * 512].rearrange(
                    "(c b p) f -> p b c f", p=128, b=2)[:, b2]
                k.dma("q" if b2 == 0 else "g", dst, srcb,
                      r=("HBMH",), w=hsubtags)
            if not shared:
                emit_bcast_row(k, E, E["R5"][0:1, tt * 512:(tt + 1) * 512], "R5",
                               E["BC"][:, 0:512], "BC", 512)
            for oc in range(8):
                b2col = (E["BAH"][:, 80 + oc:81 + oc] if shared
                         else E["BAH"][:, ei * 8 + oc:ei * 8 + oc + 1])
                if shared:
                    src_ap = io["sw2"][:, oc * 128:(oc + 1) * 128]
                else:
                    src_ap = io["rw2"][ei][:, oc * 128:(oc + 1) * 128]
                k.dma("q", E["WS"][:, 0:2048].rearrange("p (a o) -> p a o", a=16),
                      src_ap.rearrange("(a p) o -> p a o", p=128), w=("WS",))
                k.op("v", lambda e: e.tensor_copy(
                    E["WB"][:, 2048 + 0:2048 + 2048], E["WS"][:, 0:2048]),
                    r=("WS",), w=("WB",))

                def mm_o(eng):
                    for i3 in range(16):
                        inst = eng.matmul(
                            E["ps4"][0:128, 0:512],
                            E["WB"][:, 2048 + i3 * 128:2048 + (i3 + 1) * 128],
                            hsub(i3), start=(i3 == 0), stop=(i3 == 15))
                    return inst
                k.multi("t", mm_o, r=("WB",) + hsubtags, w=("ps4",))
                if shared:
                    k.op("v", lambda e, _b=b2col: e.tensor_scalar(
                        E["SC"][:, 0:512], E["ps4"][0:128, 0:512], _b, None,
                        ALU.add), r=("ps4", "BAH"), w=("SC",))
                else:
                    k.op("v", lambda e, _b=b2col: e.scalar_tensor_tensor(
                        E["SC"][:, 0:512], E["ps4"][0:128, 0:512], _b,
                        E["BC"][:, 0:512], ALU.add, ALU.mult),
                        r=("ps4", "BAH", "BC"), w=("SC",))
                k.op("v", lambda e, _oc=oc, _tt=tt: e.tensor_add(
                    xr[_oc][:, _tt * 512:(_tt + 1) * 512],
                    xr[_oc][:, _tt * 512:(_tt + 1) * 512], E["SC"][:, 0:512]),
                    r=("SC", f"XR{oc}"), w=(f"XR{oc}",))


def io_rw_tile(E, c):
    return E["RWT_ap"][c]


def emit_l1_ffn_pre(k, E, io):
    aps = []
    for c in range(NCH):
        k.dma("q", E["SCL"][:, 136 + c * 8:136 + (c + 1) * 8],
              io["router_w"][c * 128:(c + 1) * 128, :], w=("RWT", "SCL"))
        aps.append(E["SCL"][:, 136 + c * 8:136 + (c + 1) * 8])
    E["RWT_ap"] = aps


def build_l1_ffn_program():
    nc = bass.Bass("TRN2", target_bir_lowering=False, debug=False,
                   detect_race_conditions=False)
    x_in = nc.dram_tensor("x_in", [D, TC], F32, kind="ExternalInput").ap()
    kon = nc.dram_tensor("kon", [128, 140], F32, kind="ExternalInput").ap()
    io = {}
    io["router_w"] = nc.dram_tensor("router_w", [1024, 8], F32, kind="ExternalInput").ap()
    io["sw1"] = nc.dram_tensor("sw1", [1024, 2048], F32, kind="ExternalInput").ap()
    io["sw2"] = nc.dram_tensor("sw2", [2048, 1024], F32, kind="ExternalInput").ap()
    io["rw1"] = nc.dram_tensor("rw1", [8, 1024, 2048], F32, kind="ExternalInput").ap()
    io["rw2"] = nc.dram_tensor("rw2", [8, 2048, 1024], F32, kind="ExternalInput").ap()
    io["scl"] = nc.dram_tensor("scl", [128, 224], F32, kind="ExternalInput").ap()
    io["rb1c"] = nc.dram_tensor("rb1c", [128, 128], F32, kind="ExternalInput").ap()
    io["rb2c"] = nc.dram_tensor("rb2c", [128, 128], F32, kind="ExternalInput").ap()
    io["h1s"] = nc.dram_tensor("h1s", [2048, 2048], BF16, kind="Internal").ap()
    x_out = nc.dram_tensor("x_out", [D, TC], F32, kind="ExternalOutput").ap()
    dbg = nc.dram_tensor("dbg", [128, 136], F32, kind="ExternalOutput").ap()
    es = ExitStack()
    k = K(nc)
    with es:
        k.start(es)
        E = alloc_env(k)
        emit_env_init(k, E, kon)
        emit_load_resid(k, E, x_in)
        emit_l1_ffn(k, E, io)
        k.op("v", lambda e: e.tensor_copy(E["SC"][:, 0:128], E["AR"][:, 2048:2176]),
             r=("MW",), w=("SC",))
        k.dma("g", dbg[:, 0:128], E["SC"][:, 0:128], r=("SC",))
        k.dma("g", dbg[0:1, 128:132].rearrange("a b -> a b"), E["R1"][0:1, 0:4], r=("R1",))
        k.dma("g", dbg[0:1, 132:136], E["R4"][0:1, 0:4], r=("R4",))
        emit_store_resid(k, E, x_out)
        k.finalize()
    return nc


def l1_ffn_host(inputs):
    scl = np.zeros((128, 224), np.float32)
    scl[:, 208:216] = np.tile(inputs["L1_ffn_router_b"][None, :], (128, 1))
    rb1c = np.zeros((128, 128), np.float32)
    rb1c[:, 0:128] = inputs["L1_ffn_rb1"].reshape(8, 16, 128).transpose(2, 0, 1).reshape(128, 128)
    rb2c = np.zeros((128, 128), np.float32)
    rb2c[:, 0:64] = inputs["L1_ffn_rb2"].reshape(8, 8, 128).transpose(2, 0, 1).reshape(128, 64)
    rb2c[:, 64:80] = inputs["L1_ffn_sb1"].reshape(16, 128).T
    rb2c[:, 80:88] = inputs["L1_ffn_sb2"].reshape(8, 128).T
    return {"router_w": _c(inputs["L1_ffn_router_w"]),
            "sw1": _c(inputs["L1_ffn_sw1"]), "sw2": _c(inputs["L1_ffn_sw2"]),
            "rw1": _c(inputs["L1_ffn_rw1"]), "rw2": _c(inputs["L1_ffn_rw2"]),
            "scl": scl, "rb1c": rb1c, "rb2c": rb2c, "kon": konst_host()}


def run_l1_ffn(x_cores, inputs):
    nc = build_l1_ffn_program()
    hin = l1_ffn_host(inputs)
    in_maps = [dict(hin, x_in=x_cores[c]) for c in range(NCORE)]
    res = run_bass_kernel_spmd(nc, in_maps, core_ids=list(range(NCORE)))
    return [res.results[c]["x_out"] for c in range(NCORE)]


def emit_lpbq_weights(k, E, w_ap, qT_ap, orow_tiles, kcols):
    nblk = kcols // 32
    nh = (kcols + 1535) // 1536
    for rt in range(orow_tiles):
        for hh in range(nh):
            c0 = hh * 1536
            c1 = min(kcols, c0 + 1536)
            nn = c1 - c0
            nb = nn // 32
            k.dma("q", E["WS"][:, 0:nn], w_ap[rt * 128:(rt + 1) * 128, c0:c1],
                  w=("WS",))
            wv = E["WS"][:, 0:nn].rearrange("p (b x) -> p b x", x=32)
            k.op("v", lambda e, _wv=wv, _nb=nb: e.reduce_max(
                E["SC"][:, 0:_nb].unsqueeze(2), _wv, axis=AX.X,
                apply_absolute_value=True), r=("WS",), w=("SC",))
            k.op("v", lambda e, _nb=nb: e.tensor_scalar(
                E["SC"][:, 0:_nb], E["SC"][:, 0:_nb], 1.0 / 7.0, 1e-12,
                ALU.mult, ALU.max), r=("SC",), w=("SC",))
            k.op("v", lambda e, _nb=nb: e.reciprocal(
                E["SC"][:, 64:64 + _nb], E["SC"][:, 0:_nb]), r=("SC",), w=("SC",))
            sb = E["SC"][:, 0:nb].unsqueeze(2).broadcast_to([128, nb, 32])
            rsb = E["SC"][:, 64:64 + nb].unsqueeze(2).broadcast_to([128, nb, 32])
            t_ap = E["SC"][:, 512:512 + nn]
            k.op("v", lambda e, _wv=wv, _r=rsb, _t=t_ap, _nn=nn: e.tensor_tensor(
                _t.rearrange("p (b x) -> p b x", x=32), _wv, _r, ALU.mult),
                r=("WS", "SC"), w=("SC",))
            k.op("v", lambda e, _t=t_ap: e.tensor_scalar(
                _t, _t, MAGIC, MAGIC, ALU.add, ALU.subtract), r=("SC",), w=("SC",))
            k.op("v", lambda e, _t=t_ap, _s=sb, _nn=nn: e.tensor_tensor(
                E["SB2"][:, 0:_nn].rearrange("p (b x) -> p b x", x=32),
                _t.rearrange("p (b x) -> p b x", x=32), _s, ALU.mult),
                r=("SC",), w=("SB2",))
            ntile = nn // 128
            for i4 in range(0, ntile, 4):
                nsub = min(4, ntile - i4)
                for s4 in range(nsub):
                    k.multi("t", lambda e, _i=i4 + s4: e.transpose(
                        E["psT"][0:128, 0:128],
                        E["SB2"][:, _i * 128:(_i + 1) * 128], E["IDB"][:]),
                        r=("SB2", "IDB"), w=("psT",))
                    k.op("s", lambda e, _s4=s4: e.copy(
                        E["TMX"][:, _s4 * 128:(_s4 + 1) * 128],
                        E["psT"][0:128, 0:128]), r=("psT",), w=("TMX",))
                k.dma("g", qT_ap[c0 + i4 * 128:c0 + (i4 + nsub) * 128,
                                 rt * 128:(rt + 1) * 128].rearrange(
                                     "(a p) o -> p a o", p=128),
                      E["TMX"][:, 0:nsub * 128].rearrange("p (a o) -> p a o",
                                                          o=128),
                      r=("TMX",), w=("HBMQ",))


def emit_gmax_track(k, E, src_ap, src_tag, first):
    k.op("v", lambda e: e.reduce_max(
        E["CT"][:, 24:25], src_ap, axis=AX.X, apply_absolute_value=True),
        r=(src_tag,), w=("CT",))
    if first:
        k.op("v", lambda e: e.tensor_copy(E["CT"][:, 25:26], E["CT"][:, 24:25]),
             r=("CT",), w=("CT",))
    else:
        k.op("v", lambda e: e.tensor_max(
            E["CT"][:, 25:26], E["CT"][:, 25:26], E["CT"][:, 24:25]),
            r=("CT",), w=("CT",))


def emit_gmax_scales(k, E, sc_col, rsc_col):
    k.multi("t", lambda e: e.transpose(
        E["ps5"][0:1, 0:128], E["CT"][:, 25:26], E["IDF"][:]),
        r=("CT", "IDF"), w=("ps5",))
    k.op("v", lambda e: e.reduce_max(
        E["SC"][0:1, 0:1], E["ps5"][0:1, 0:128], axis=AX.X), r=("ps5",), w=("SC",))
    k.op("v", lambda e: e.tensor_scalar(
        E["SC"][0:1, 0:1], E["SC"][0:1, 0:1], 1.0 / 32767.0, 1e-12,
        ALU.mult, ALU.max), r=("SC",), w=("SC",))
    k.op("v", lambda e: e.reciprocal(E["SC"][0:1, 1:2], E["SC"][0:1, 0:1]),
         r=("SC",), w=("SC",))
    for col, s0 in ((sc_col, 0), (rsc_col, 1)):
        k.multi("t", lambda e, _c=col, _s=s0: e.matmul(
            E["psB"][0:128, 0:1], E["ONE1F"][0:1, :], E["SC"][0:1, _s:_s + 1],
            start=True, stop=True), r=("SC", "ONE1F"), w=("psB",))
        k.op("s", lambda e, _c=col: e.copy(E["CT"][:, _c:_c + 1],
                                           E["psB"][0:128, 0:1]),
             r=("psB",), w=("CT",))


def emit_l3_ffn(k, E, io):
    xr = [E["XR"][:, c * TC:(c + 1) * TC] for c in range(NCH)]
    ln = [E["LNB"][:, c * S:c * S + TC] for c in range(NCH)]
    lntags = tuple(f"LN{c}" for c in range(NCH))
    xrtags = tuple(f"XR{c}" for c in range(NCH))
    NOC = 24

    def hsub(i):
        base = (i // 3) * S + TC + (i % 3) * 512
        return E["LNB"][:, base:base + 512]
    hsubtags = tuple(f"HS{i}" for i in range(NOC))

    emit_ln(k, E, xr, ln, TC, xrtags, lntags)
    emit_lpbq_weights(k, E, io["wg"], io["qg"], 24, 1024)
    emit_lpbq_weights(k, E, io["wu"], io["qu"], 24, 1024)
    emit_lpbq_weights(k, E, io["wd"], io["qd"], 8, 3072)
    for ci in range(NCH):
        emit_gmax_track(k, E, ln[ci][:, 0:TC], f"LN{ci}", ci == 0)
    emit_gmax_scales(k, E, 26, 27)
    for ci in range(NCH):
        k.op("v", lambda e, _c=ci: e.tensor_scalar(
            E["SC"][:, 0:TC], ln[_c][:, 0:TC], E["CT"][:, 27:28], MAGIC,
            ALU.mult, ALU.add), r=lntags + ("CT",), w=("SC",))
        k.op("v", lambda e, _c=ci: e.tensor_scalar(
            ln[_c][:, 0:TC], E["SC"][:, 0:TC], MAGIC, E["CT"][:, 26:27],
            ALU.subtract, ALU.mult), r=("SC", "CT"), w=lntags)
    for pname, qw, outn, mxcols in (("up", "qu", "ups", (28, 29)),
                                    ("gate", "qg", "gts", (30, 31))):
        first = True
        for tt in range(4):
            for oc in range(NOC):
                k.dma("q", E["WB"][:, 0:1024].rearrange("p (a o) -> p a o", a=8),
                      io[qw][:, oc * 128:(oc + 1) * 128].rearrange(
                          "(a p) o -> p a o", p=128), r=("HBMQ",), w=("WB",))

                def mm_p(eng, _tt=tt):
                    for kcc in range(8):
                        inst = eng.matmul(
                            E["ps0"][0:128, 0:512],
                            E["WB"][:, kcc * 128:(kcc + 1) * 128],
                            ln[kcc][:, _tt * 512:(_tt + 1) * 512],
                            start=(kcc == 0), stop=(kcc == 7))
                    return inst
                k.multi("t", mm_p, r=("WB",) + lntags, w=("ps0",))
                k.op("s", lambda e: e.copy(E["SC"][:, 0:512], E["ps0"][0:128, 0:512]),
                     r=("ps0",), w=("SC",))
                emit_gmax_track(k, E, E["SC"][:, 0:512], "SC", first)
                first = False
                k.dma("g", io[outn][oc * 128:(oc + 1) * 128,
                                    tt * 512:(tt + 1) * 512],
                      E["SC"][:, 0:512], r=("SC",), w=("HBMH",))
        emit_gmax_scales(k, E, *mxcols)
    first = True
    for tt in range(4):
        for oc in range(NOC):
            k.dma("q", E["SC"][:, 0:512],
                  io["gts"][oc * 128:(oc + 1) * 128, tt * 512:(tt + 1) * 512],
                  r=("HBMH",), w=("SC",))
            k.op("v", lambda e: e.tensor_scalar(
                E["SC"][:, 512:1024], E["SC"][:, 0:512], E["CT"][:, 31:32],
                MAGIC, ALU.mult, ALU.add), r=("SC", "CT"), w=("SC",))
            k.op("v", lambda e: e.tensor_scalar(
                E["SC"][:, 512:1024], E["SC"][:, 512:1024], MAGIC,
                E["CT"][:, 30:31], ALU.subtract, ALU.mult), r=("SC", "CT"), w=("SC",))
            k.op("s", lambda e: e.activation(
                E["SC"][:, 1024:1536], E["SC"][:, 512:1024], AF.Sigmoid),
                r=("SC",), w=("SC",))
            k.op("v", lambda e: e.tensor_scalar(
                E["SC"][:, 1024:1536], E["SC"][:, 1024:1536], 65536.0, MAGIC,
                ALU.mult, ALU.add), r=("SC",), w=("SC",))
            k.op("v", lambda e: e.tensor_scalar(
                E["SC"][:, 1024:1536], E["SC"][:, 1024:1536], MAGIC, 65535.0,
                ALU.subtract, ALU.min), r=("SC",), w=("SC",))
            k.op("v", lambda e: e.tensor_scalar(
                E["SC"][:, 1024:1536], E["SC"][:, 1024:1536], 1.0 / 65536.0,
                None, ALU.mult), r=("SC",), w=("SC",))
            k.op("v", lambda e: e.tensor_mul(
                E["SC"][:, 0:512], E["SC"][:, 512:1024], E["SC"][:, 1024:1536]),
                r=("SC",), w=("SC",))
            emit_gmax_track(k, E, E["SC"][:, 0:512], "SC", first)
            first = False
            k.dma("g", io["gts"][oc * 128:(oc + 1) * 128, tt * 512:(tt + 1) * 512],
                  E["SC"][:, 0:512], r=("SC",), w=("HBMH",))
    emit_gmax_scales(k, E, 30, 31)
    first = True
    for tt in range(4):
        for oc in range(NOC):
            k.dma("q", E["SC"][:, 0:512],
                  io["gts"][oc * 128:(oc + 1) * 128, tt * 512:(tt + 1) * 512],
                  r=("HBMH",), w=("SC",))
            k.dma("g", E["SC"][:, 512:1024],
                  io["ups"][oc * 128:(oc + 1) * 128, tt * 512:(tt + 1) * 512],
                  r=("HBMH",), w=("SC",))
            k.op("v", lambda e: e.tensor_scalar(
                E["SC"][:, 1024:1536], E["SC"][:, 0:512], E["CT"][:, 31:32],
                MAGIC, ALU.mult, ALU.add), r=("SC", "CT"), w=("SC",))
            k.op("v", lambda e: e.tensor_scalar(
                E["SC"][:, 1024:1536], E["SC"][:, 1024:1536], MAGIC,
                E["CT"][:, 30:31], ALU.subtract, ALU.mult), r=("SC", "CT"), w=("SC",))
            k.op("v", lambda e: e.tensor_scalar(
                E["SC"][:, 1536:2048], E["SC"][:, 512:1024], E["CT"][:, 29:30],
                MAGIC, ALU.mult, ALU.add), r=("SC", "CT"), w=("SC",))
            k.op("v", lambda e: e.tensor_scalar(
                E["SC"][:, 1536:2048], E["SC"][:, 1536:2048], MAGIC,
                E["CT"][:, 28:29], ALU.subtract, ALU.mult), r=("SC", "CT"), w=("SC",))
            k.op("v", lambda e: e.tensor_mul(
                E["SC"][:, 0:512], E["SC"][:, 1024:1536], E["SC"][:, 1536:2048]),
                r=("SC",), w=("SC",))
            emit_gmax_track(k, E, E["SC"][:, 0:512], "SC", first)
            first = False
            k.dma("g", io["ups"][oc * 128:(oc + 1) * 128, tt * 512:(tt + 1) * 512],
                  E["SC"][:, 0:512], r=("SC",), w=("HBMH",))
    emit_gmax_scales(k, E, 28, 29)
    for tt in range(4):
        for oc in range(NOC):
            k.dma("q", E["SC"][:, 0:512],
                  io["ups"][oc * 128:(oc + 1) * 128, tt * 512:(tt + 1) * 512],
                  r=("HBMH",), w=("SC",))
            k.op("v", lambda e: e.tensor_scalar(
                E["SC"][:, 512:1024], E["SC"][:, 0:512], E["CT"][:, 29:30],
                MAGIC, ALU.mult, ALU.add), r=("SC", "CT"), w=("SC",))
            k.op("v", lambda e, _oc=oc: e.tensor_scalar(
                hsub(_oc), E["SC"][:, 512:1024], MAGIC, E["CT"][:, 28:29],
                ALU.subtract, ALU.mult), r=("SC", "CT"), w=(f"HS{oc}",))
        for ocd in range(8):
            for j3 in range(3):
                k.dma("q" if j3 % 2 == 0 else "g",
                      E["WB"][:, j3 * 1024:(j3 + 1) * 1024].rearrange(
                          "p (a o) -> p a o", a=8),
                      io["qd"][j3 * 1024:(j3 + 1) * 1024,
                               ocd * 128:(ocd + 1) * 128].rearrange(
                          "(a p) o -> p a o", p=128), r=("HBMQ",), w=("WB",))

            def mm_d(eng):
                for i3 in range(NOC):
                    inst = eng.matmul(
                        E["ps2"][0:128, 0:512],
                        E["WB"][:, i3 * 128:(i3 + 1) * 128], hsub(i3),
                        start=(i3 == 0), stop=(i3 == NOC - 1))
                return inst
            k.multi("t", mm_d, r=("WB",) + hsubtags, w=("ps2",))
            k.op("s", lambda e: e.copy(E["SC"][:, 0:512], E["ps2"][0:128, 0:512]),
                 r=("ps2",), w=("SC",))
            k.op("v", lambda e, _o=ocd, _tt=tt: e.tensor_add(
                xr[_o][:, _tt * 512:(_tt + 1) * 512],
                xr[_o][:, _tt * 512:(_tt + 1) * 512], E["SC"][:, 0:512]),
                r=("SC", f"XR{ocd}"), w=(f"XR{ocd}",))


def build_l3_ffn_program():
    nc = bass.Bass("TRN2", target_bir_lowering=False, debug=False,
                   detect_race_conditions=False)
    x_in = nc.dram_tensor("x_in", [D, TC], F32, kind="ExternalInput").ap()
    kon = nc.dram_tensor("kon", [128, 140], F32, kind="ExternalInput").ap()
    io = {}
    io["wg"] = nc.dram_tensor("wg", [3072, 1024], F32, kind="ExternalInput").ap()
    io["wu"] = nc.dram_tensor("wu", [3072, 1024], F32, kind="ExternalInput").ap()
    io["wd"] = nc.dram_tensor("wd", [1024, 3072], F32, kind="ExternalInput").ap()
    io["qg"] = nc.dram_tensor("qg", [1024, 3072], BF16, kind="Internal").ap()
    io["qu"] = nc.dram_tensor("qu", [1024, 3072], BF16, kind="Internal").ap()
    io["qd"] = nc.dram_tensor("qd", [3072, 1024], BF16, kind="Internal").ap()
    io["ups"] = nc.dram_tensor("ups", [3072, 2048], F32, kind="Internal").ap()
    io["gts"] = nc.dram_tensor("gts", [3072, 2048], F32, kind="Internal").ap()
    x_out = nc.dram_tensor("x_out", [D, TC], F32, kind="ExternalOutput").ap()
    es = ExitStack()
    k = K(nc)
    with es:
        k.start(es)
        E = alloc_env(k)
        emit_env_init(k, E, kon)
        emit_load_resid(k, E, x_in)
        emit_l3_ffn(k, E, io)
        emit_store_resid(k, E, x_out)
        k.finalize()
    return nc


def run_l3_ffn(x_cores, inputs):
    nc = build_l3_ffn_program()
    hin = {"wg": _c(inputs["L3_ffn_w_gate"]), "wu": _c(inputs["L3_ffn_w_up"]),
           "wd": _c(inputs["L3_ffn_w_down"]), "kon": konst_host()}
    in_maps = [dict(hin, x_in=x_cores[c]) for c in range(NCORE)]
    res = run_bass_kernel_spmd(nc, in_maps, core_ids=list(range(NCORE)))
    return [res.results[c]["x_out"] for c in range(NCORE)]


def emit_l3a1(k, E, io):
    xr = [E["XR"][:, c * TC:(c + 1) * TC] for c in range(NCH)]
    lnf = [E["LNB"][:, c * S:(c + 1) * S] for c in range(NCH)]
    lntags = tuple(f"LN{c}" for c in range(NCH))
    xrtags = tuple(f"XR{c}" for c in range(NCH))
    own = [lnf[c][:, 0:TC] for c in range(NCH)]
    prt = [lnf[c][:, TC:S] for c in range(NCH)]
    emit_ln(k, E, xr, own, TC, xrtags, lntags)
    for c in range(NCH):
        k.dma("q", E["WS"][:, 0:2048], io["x_partner"][c * 128:(c + 1) * 128, :],
              w=("WS",))
        k.op("v", lambda e, _c=c: e.tensor_copy(prt[_c], E["WS"][:, 0:2048]),
             r=("WS",), w=lntags)
    emit_ln(k, E, prt, prt, TC, lntags, lntags, src_bf16=True)
    k.dma("q", E["SCL"][:, 0:224], io["scl"], w=("SCL",))

    for oc in range(32):
        k.dma("q", E["WS"][:, 0:1024].rearrange("p (a o) -> p a o", a=8),
              io["in_wT"][:, oc * 128:(oc + 1) * 128].rearrange(
                  "(a p) o -> p a o", p=128), w=("WS",))
        k.op("v", lambda e: e.tensor_copy(E["WB"][:, 0:1024], E["WS"][:, 0:1024]),
             r=("WS",), w=("WB",))
        for tt in range(8):
            def mm_ip(eng, _tt=tt):
                for c in range(NCH):
                    inst = eng.matmul(
                        E["ps0"][0:128, 0:512], E["WB"][:, c * 128:(c + 1) * 128],
                        lnf[c][:, _tt * 512:(_tt + 1) * 512], start=(c == 0),
                        stop=(c == 7))
                return inst
            k.multi("t", mm_ip, r=("WB",) + lntags, w=("ps0",))
            k.op("s", lambda e: e.copy(E["SB2"][:, 0:512], E["ps0"][0:128, 0:512]),
                 r=("ps0",), w=("SB2",))
            dstt = io["xis"] if oc < 16 else io["zs"]
            row0 = (oc % 16) * 128
            k.dma("g", dstt[row0:row0 + 128, tt * 512:(tt + 1) * 512],
                  E["SB2"][:, 0:512], r=("SB2",), w=("HBMH",))
    XIB = E["AR"][:, 0:4104]
    XCB = E["AR"][:, 4104:8200]
    for ch in range(16):
        k.op("v", lambda e: e.memset(XIB[:, 0:3], 0.0), w=("ARS",))
        k.dma("q", XIB[:, 3:4099], io["xis"][ch * 128:(ch + 1) * 128, :],
              r=("HBMH",), w=("ARS",))
        for q4 in range(4):
            k.op("v", lambda e, _q=q4, _ch=ch: e.tensor_scalar(
                E["SC"][:, 0:1024], XIB[:, _q * 1024:_q * 1024 + 1024],
                E["SCL"][:, _ch * 4:_ch * 4 + 1], None, ALU.mult),
                r=("ARS", "SCL"), w=("SC",))
            for tap in range(1, 4):
                k.op("v", lambda e, _q=q4, _ch=ch, _k2=tap: e.scalar_tensor_tensor(
                    E["SC"][:, 0:1024],
                    XIB[:, _q * 1024 + _k2:_q * 1024 + 1024 + _k2],
                    E["SCL"][:, _ch * 4 + _k2:_ch * 4 + _k2 + 1],
                    E["SC"][:, 0:1024], ALU.mult, ALU.add),
                    r=("ARS", "SCL", "SC"), w=("SC",))
            k.op("s", lambda e, _q=q4, _ch=ch: e.activation(
                XCB[:, _q * 1024:(_q + 1) * 1024], E["SC"][:, 0:1024], AF.Silu,
                bias=E["SCL"][:, 64 + _ch:65 + _ch]), r=("SC", "SCL"), w=("ARS",))
        k.dma("g", io["xcs"][ch * 128:(ch + 1) * 128, :], XCB[:, 0:4096],
              r=("ARS",), w=("HBMH",))
    for tt in range(8):
        for ch in range(16):
            k.dma("q", E["WS"][:, 0:96], io["xwT"][ch * 128:(ch + 1) * 128, :],
                  w=("WS",))
            k.op("v", lambda e: e.tensor_copy(E["WB"][:, 0:96], E["WS"][:, 0:96]),
                 r=("WS",), w=("WB",))
            k.dma("g", E["SB2"][:, 0:512],
                  io["xcs"][ch * 128:(ch + 1) * 128, tt * 512:(tt + 1) * 512],
                  r=("HBMH",), w=("SB2",))
            k.multi("t", lambda e, _ch=ch: e.matmul(
                E["ps0"][0:64, 0:512], E["WB"][:, 0:64], E["SB2"][:, 0:512],
                start=(_ch == 0), stop=(_ch == 15)), r=("WB", "SB2"), w=("ps0",))
            k.multi("t", lambda e, _ch=ch: e.matmul(
                E["ps1"][0:16, 0:512], E["WB"][:, 64:80], E["SB2"][:, 0:512],
                start=(_ch == 0), stop=(_ch == 15)), r=("WB", "SB2"), w=("ps1",))
            k.multi("t", lambda e, _ch=ch: e.matmul(
                E["ps2"][0:16, 0:512], E["WB"][:, 80:96], E["SB2"][:, 0:512],
                start=(_ch == 0), stop=(_ch == 15)), r=("WB", "SB2"), w=("ps2",))
        k.op("s", lambda e: e.copy(E["SB2"][0:64, 512:1024], E["ps0"][0:64, 0:512]),
             r=("ps0",), w=("SB2",))
        k.op("s", lambda e: e.copy(E["SB2"][0:16, 1024:1536],
                                   E["ps1"][0:16, 0:512]), r=("ps1",), w=("SB2",))
        k.op("v", lambda e: e.tensor_copy(E["SB2"][0:16, 0:512],
                                          E["ps2"][0:16, 0:512]),
             r=("ps2",), w=("SB2",))
        k.dma("g", io["bcs"][0:16, tt * 512:(tt + 1) * 512],
              E["SB2"][0:16, 1024:1536], r=("SB2",), w=("HBMB",))
        k.dma("g", io["bcs"][16:32, tt * 512:(tt + 1) * 512],
              E["SB2"][0:16, 0:512], r=("SB2",), w=("HBMB",))
        for oc in range(16):
            k.dma("q", E["WS"][0:64, 0:128],
                  io["dtwT"][:, oc * 128:(oc + 1) * 128], w=("WS",))
            k.op("v", lambda e: e.tensor_copy(E["WB"][0:64, 128:256],
                                              E["WS"][0:64, 0:128]),
                 r=("WS",), w=("WB",))
            k.multi("t", lambda e: e.matmul(
                E["ps3"][0:128, 0:512], E["WB"][0:64, 128:256],
                E["SB2"][0:64, 512:1024], start=True, stop=True),
                r=("WB", "SB2"), w=("ps3",))
            k.op("v", lambda e, _oc=oc: e.tensor_scalar(
                E["SC"][:, 0:512], E["ps3"][0:128, 0:512],
                E["SCL"][:, 80 + _oc:81 + _oc], None, ALU.add),
                r=("ps3", "SCL"), w=("SC",))
            k.op("s", lambda e: e.activation(
                E["SC"][:, 512:1024], E["SC"][:, 0:512], AF.Abs),
                r=("SC",), w=("SC",))
            k.op("s", lambda e: e.activation(
                E["SC"][:, 512:1024], E["SC"][:, 512:1024], AF.Exp,
                scale=kc(E, 5)), r=("SC", "KONF"), w=("SC",))
            k.op("s", lambda e: e.activation(
                E["SC"][:, 512:1024], E["SC"][:, 512:1024], AF.Ln,
                bias=kc(E, 0)), r=("SC", "KONF"), w=("SC",))
            k.op("v", lambda e: e.tensor_scalar(
                E["SC"][:, 0:512], E["SC"][:, 0:512], 0.0, None, ALU.max),
                r=("SC",), w=("SC",))
            k.op("v", lambda e: e.tensor_add(
                E["SC"][:, 0:512], E["SC"][:, 0:512], E["SC"][:, 512:1024]),
                r=("SC",), w=("SC",))
            k.op("v", lambda e: e.tensor_copy(E["SB2"][:, 1024:1536],
                                              E["SC"][:, 0:512]),
                 r=("SC",), w=("SB2",))
            k.dma("g", io["dts"][oc * 128:(oc + 1) * 128, tt * 512:(tt + 1) * 512],
                  E["SB2"][:, 1024:1536], r=("SB2",), w=("HBMH",))
    for ch in range(16):
        k.dma("q", E["BA"][:, 0:16], io["A_log"][ch * 128:(ch + 1) * 128, :],
              w=("BA",))
        k.op("s", lambda e: e.activation(E["BA"][:, 16:32], E["BA"][:, 0:16],
                                         AF.Exp), r=("BA",), w=("BA",))
        k.op("v", lambda e: e.tensor_scalar(
            E["BA"][:, 16:32], E["BA"][:, 16:32], -1.0, None, ALU.mult),
            r=("BA",), w=("BA",))
        k.dma("q", E["AR"][:, 0:4096], io["dts"][ch * 128:(ch + 1) * 128, :],
              r=("HBMH",), w=("ARS",))
        for q4 in range(4):
            k.dma("g", E["SB2"][:, 0:1024],
                  io["xcs"][ch * 128:(ch + 1) * 128, q4 * 1024:(q4 + 1) * 1024],
                  r=("HBMH",), w=("SB2",))
            k.op("v", lambda e, _q=q4: e.tensor_mul(
                E["AR"][:, 4096 + _q * 1024:4096 + (_q + 1) * 1024],
                E["AR"][:, _q * 1024:(_q + 1) * 1024], E["SB2"][:, 0:1024]),
                r=("ARS", "SB2"), w=("ARS",))
        for q4 in range(4):
            tsl = slice(q4 * 1024, (q4 + 1) * 1024)
            for s in range(16):
                k.dma("q", E["R2"][0:1, 0:1024], io["bcs"][s:s + 1, tsl],
                      r=("HBMB",), w=("R2",))
                k.dma("g", E["R5"][0:1, 0:1024], io["bcs"][16 + s:17 + s, tsl],
                      r=("HBMB",), w=("R5",))
                for hh in range(2):
                    k.multi("t", lambda e, _h=hh: e.matmul(
                        (E["psB"] if _h == 0 else E["ps5"])[0:128, 0:512],
                        E["ONE1B"][0:1, :],
                        E["R2"][0:1, _h * 512:(_h + 1) * 512],
                        start=True, stop=True), r=("R2", "ONE1B"),
                        w=("psB" if hh == 0 else "ps5",))
                    k.multi("t", lambda e, _h=hh: e.matmul(
                        (E["ps3"] if _h == 0 else E["ps4"])[0:128, 0:512],
                        E["ONE1B"][0:1, :],
                        E["R5"][0:1, _h * 512:(_h + 1) * 512],
                        start=True, stop=True), r=("R5", "ONE1B"),
                        w=("ps3" if hh == 0 else "ps4",))
                k.op("s", lambda e, _q=q4, _s=s: e.activation(
                    E["BC"][:, 0:1024], E["AR"][:, _q * 1024:(_q + 1) * 1024],
                    AF.Exp, scale=E["BA"][:, 16 + _s:17 + _s]),
                    r=("ARS", "BA"), w=("BC",))
                k.op("v", lambda e, _q=q4: e.tensor_mul(
                    E["WS"][:, 1024:1536],
                    E["AR"][:, 4096 + _q * 1024:4096 + _q * 1024 + 512],
                    E["psB"][0:128, 0:512]), r=("ARS", "psB"), w=("WS",))
                k.op("v", lambda e, _q=q4: e.tensor_mul(
                    E["WS"][:, 1536:2048],
                    E["AR"][:, 4096 + _q * 1024 + 512:4096 + (_q + 1) * 1024],
                    E["ps5"][0:128, 0:512]), r=("ARS", "ps5"), w=("WS",))
                k.op("v", lambda e, _s=s, _q=q4: e.tensor_tensor_scan(
                    E["WS"][:, 0:512], E["BC"][:, 0:512], E["WS"][:, 1024:1536],
                    (0.0 if _q == 0 else E["BAH"][:, _s:_s + 1]),
                    ALU.mult, ALU.add), r=("BC", "WS", "BAH"), w=("WS",))
                k.op("v", lambda e: e.tensor_tensor_scan(
                    E["WS"][:, 512:1024], E["BC"][:, 512:1024],
                    E["WS"][:, 1536:2048], E["WS"][:, 511:512],
                    ALU.mult, ALU.add), r=("BC", "WS"), w=("WS",))
                k.op("v", lambda e, _s=s: e.tensor_copy(
                    E["BAH"][:, _s:_s + 1], E["WS"][:, 1023:1024]),
                    r=("WS",), w=("BAH",))
                k.op("v", lambda e: e.tensor_mul(
                    E["SB2"][:, 0:512], E["WS"][:, 0:512], E["ps3"][0:128, 0:512]),
                    r=("WS", "ps3"), w=("SB2",))
                k.op("v", lambda e: e.tensor_mul(
                    E["SB2"][:, 512:1024], E["WS"][:, 512:1024],
                    E["ps4"][0:128, 0:512]), r=("WS", "ps4"), w=("SB2",))
                k.multi("t", lambda e, _s=s: e.matmul(
                    E["ps1"][0:128, 0:512], E["IDB"][:], E["SB2"][:, 0:512],
                    start=(_s == 0), stop=(_s == 15)), r=("IDB", "SB2"), w=("ps1",))
                k.multi("t", lambda e, _s=s: e.matmul(
                    E["ps2"][0:128, 0:512], E["IDB"][:], E["SB2"][:, 512:1024],
                    start=(_s == 0), stop=(_s == 15)), r=("IDB", "SB2"), w=("ps2",))
            for hh in range(2):
                psy = "ps1" if hh == 0 else "ps2"
                off = q4 * 1024 + hh * 512
                k.dma("q", E["SB2"][:, 0:512],
                      io["xcs"][ch * 128:(ch + 1) * 128, off:off + 512],
                      r=("HBMH",), w=("SB2",))
                k.dma("g", E["SB2"][:, 512:1024],
                      io["zs"][ch * 128:(ch + 1) * 128, off:off + 512],
                      r=("HBMH",), w=("SB2",))
                k.op("v", lambda e, _p=psy, _ch=ch: e.scalar_tensor_tensor(
                    E["SC"][:, 0:512], E["SB2"][:, 0:512],
                    E["SCL"][:, 96 + _ch:97 + _ch], E[_p][0:128, 0:512],
                    ALU.mult, ALU.add), r=("SB2", "SCL", psy), w=("SC",))
                k.op("s", lambda e: e.activation(
                    E["SB2"][:, 1024:1536], E["SB2"][:, 512:1024], AF.Silu),
                    r=("SB2",), w=("SB2",))
                k.op("v", lambda e: e.tensor_mul(
                    E["SB2"][:, 0:512], E["SC"][:, 0:512], E["SB2"][:, 1024:1536]),
                    r=("SC", "SB2"), w=("SB2",))
                k.dma("g", io["ys"][ch * 128:(ch + 1) * 128, off:off + 512],
                      E["SB2"][:, 0:512], r=("SB2",), w=("HBMH",))
    for tt in range(8):
        for c in range(16):
            k.dma("q" if c % 2 == 0 else "g",
                  E["AR"][:, c * 512:(c + 1) * 512],
                  io["ys"][c * 128:(c + 1) * 128, tt * 512:(tt + 1) * 512],
                  r=("HBMH",), w=("ARS",))
        for oc in range(8):
            k.dma("q", E["WS"][:, 0:1024].rearrange("p (a o) -> p a o", a=8),
                  io["out_wT"][0:1024, oc * 128:(oc + 1) * 128].rearrange(
                      "(a p) o -> p a o", p=128), w=("WS",))
            k.op("v", lambda e: e.tensor_copy(E["WB"][:, 0:1024], E["WS"][:, 0:1024]),
                 r=("WS",), w=("WB",))
            k.dma("g", E["WS"][:, 0:1024].rearrange("p (a o) -> p a o", a=8),
                  io["out_wT"][1024:2048, oc * 128:(oc + 1) * 128].rearrange(
                      "(a p) o -> p a o", p=128), r=("WB",), w=("WS",))
            k.op("v", lambda e: e.tensor_copy(E["WB"][:, 1024:2048],
                                              E["WS"][:, 0:1024]),
                 r=("WS",), w=("WB",))

            def mm_op(eng):
                for c in range(16):
                    inst = eng.matmul(
                        E["ps0"][0:128, 0:512], E["WB"][:, c * 128:(c + 1) * 128],
                        E["AR"][:, c * 512:(c + 1) * 512], start=(c == 0),
                        stop=(c == 15))
                return inst
            k.multi("t", mm_op, r=("WB", "ARS"), w=("ps0",))
            k.op("s", lambda e: e.copy(E["SC"][:, 0:512], E["ps0"][0:128, 0:512]),
                 r=("ps0",), w=("SC",))
            k.dma("g", io["ydir"][oc * 128:(oc + 1) * 128, tt * 512:(tt + 1) * 512],
                  E["SC"][:, 0:512], r=("SC",), w=("HBMC",))


def build_l3a1_program():
    nc = bass.Bass("TRN2", target_bir_lowering=False, debug=False,
                   detect_race_conditions=False)
    x_in = nc.dram_tensor("x_in", [D, TC], F32, kind="ExternalInput").ap()
    x_prt = nc.dram_tensor("x_prt", [D, TC], F32, kind="ExternalInput").ap()
    kon = nc.dram_tensor("kon", [128, 140], F32, kind="ExternalInput").ap()
    io = {"x_partner": x_prt}
    for n, shp in {"in_wT": [1024, 4096], "xwT": [2048, 96], "dtwT": [64, 2048],
                   "out_wT": [2048, 1024], "A_log": [2048, 16],
                   "scl": [128, 224]}.items():
        io[n] = nc.dram_tensor(n, shp, F32, kind="ExternalInput").ap()
    for n, shp, dt_ in (("xis", [2048, 4096], BF16), ("zs", [2048, 4096], BF16),
                        ("xcs", [2048, 4096], BF16), ("dts", [2048, 4096], BF16),
                        ("bcs", [32, 4096], BF16), ("ys", [2048, 4096], BF16)):
        io[n] = nc.dram_tensor(n, shp, dt_, kind="Internal").ap()
    io["ydir"] = nc.dram_tensor("ydir", [1024, 4096], F32, kind="ExternalOutput").ap()
    es = ExitStack()
    k = K(nc)
    with es:
        k.start(es)
        E = alloc_env(k)
        emit_env_init(k, E, kon)
        emit_load_resid(k, E, x_in)
        emit_l3a1(k, E, io)
        k.fence("g", k.tokens_of("HBMC"))
        k.finalize()
    return nc


def l3a1_host(inputs, direction):
    p = "L3_attn_fwd_" if direction == 0 else "L3_attn_bwd_"
    scl = np.zeros((128, 224), np.float32)
    scl[:, 0:64] = inputs[p + "conv_w"].reshape(16, 128, 4).transpose(1, 0, 2).reshape(128, 64)
    scl[:, 64:80] = inputs[p + "conv_b"].reshape(16, 128).T
    scl[:, 80:96] = inputs[p + "dt_b"].reshape(16, 128).T
    scl[:, 96:112] = inputs[p + "D"].reshape(16, 128).T
    return {"in_wT": _t(inputs[p + "in_w"]), "xwT": _t(inputs[p + "x_w"]),
            "dtwT": _t(inputs[p + "dt_w"]), "out_wT": _t(inputs[p + "out_w"]),
            "A_log": _c(inputs[p + "A_log"]), "scl": scl, "kon": konst_host()}


def run_l3a1(x_cores, inputs):
    nc = build_l3a1_program()
    hosts = [l3a1_host(inputs, d) for d in (0, 1)]
    in_maps = []
    for c in range(NCORE):
        b, h = c // 2, c % 2
        m = dict(hosts[h])
        if h == 0:
            m["x_in"] = x_cores[2 * b]
            m["x_prt"] = x_cores[2 * b + 1]
        else:
            m["x_in"] = np.ascontiguousarray(x_cores[2 * b + 1][:, ::-1])
            m["x_prt"] = np.ascontiguousarray(x_cores[2 * b][:, ::-1])
        in_maps.append(m)
    res = run_bass_kernel_spmd(nc, in_maps, core_ids=list(range(NCORE)))
    outs = []
    for c in range(NCORE):
        y = res.results[c]["ydir"]
        if c % 2 == 1:
            y = np.ascontiguousarray(y[:, ::-1])
        outs.append(y)
    return outs


def emit_l3a2(k, E, io):
    xr = [E["XR"][:, c * TC:(c + 1) * TC] for c in range(NCH)]
    ln = [E["LNB"][:, c * S:c * S + TC] for c in range(NCH)]
    lntags = tuple(f"LN{c}" for c in range(NCH))
    xrtags = tuple(f"XR{c}" for c in range(NCH))
    k.dma("q", E["SCL"][:, 0:224], io["scl"], w=("SCL",))
    emit_ln(k, E, xr, ln, TC, xrtags, lntags)
    for c in range(NCH):
        k.dma("q", E["SC"][:, 0:2048], io["yf"][c * 128:(c + 1) * 128, :],
              w=("SC",))
        k.dma("g", E["WS"][:, 0:2048], io["yb"][c * 128:(c + 1) * 128, :],
              w=("WS",))
        k.op("v", lambda e: e.tensor_add(E["SC"][:, 0:2048], E["SC"][:, 0:2048],
                                         E["WS"][:, 0:2048]),
             r=("SC", "WS"), w=("SC",))
        k.op("v", lambda e, _c=c: e.scalar_tensor_tensor(
            E["SC"][:, 0:2048], E["SC"][:, 0:2048], 0.5, ln[_c][:, 0:TC],
            ALU.mult, ALU.add), r=("SC",) + lntags, w=("SC",))
        k.dma("g", io["xts"][c * 128:(c + 1) * 128, :], E["SC"][:, 0:2048],
              r=("SC",), w=("HBMH",))
    for c in range(NCH):
        k.dma("q", E["WS"][:, 0:2048], io["xts"][c * 128:(c + 1) * 128, :],
              r=("HBMH",), w=("WS",))
        k.op("v", lambda e, _c=c: e.tensor_copy(ln[_c][:, 0:TC], E["WS"][:, 0:2048]),
             r=("WS",), w=lntags)
    gcols = [E["SCL"][:, 144 + c:145 + c] for c in range(NCH)]
    bcols = [E["SCL"][:, 152 + c:153 + c] for c in range(NCH)]
    emit_ln(k, E, ln, ln, TC, lntags, lntags, gain_bias=(gcols, bcols),
            src_bf16=True, gb_tags=("SCL",))
    def hslot(j):
        base = (j // 4) * S + TC + (j % 4) * 512
        return E["LNB"][:, base:base + 512]
    htags = tuple(f"H{j}" for j in range(32))
    for tt in range(4):
        for oc in range(32):
            k.dma("q", E["WS"][:, 0:1024].rearrange("p (a o) -> p a o", a=8),
                  io["ffw1T"][:, oc * 128:(oc + 1) * 128].rearrange(
                      "(a p) o -> p a o", p=128), w=("WS",))
            k.op("v", lambda e: e.tensor_copy(E["WB"][:, 0:1024],
                                              E["WS"][:, 0:1024]),
                 r=("WS",), w=("WB",))

            def mm_1(eng, _tt=tt):
                for c in range(NCH):
                    inst = eng.matmul(
                        E["ps0"][0:128, 0:512], E["WB"][:, c * 128:(c + 1) * 128],
                        ln[c][:, _tt * 512:(_tt + 1) * 512], start=(c == 0),
                        stop=(c == 7))
                return inst
            k.multi("t", mm_1, r=("WB",) + lntags, w=("ps0",))
            k.op("s", lambda e, _oc=oc: e.activation(
                hslot(_oc), E["ps0"][0:128, 0:512], AF.Silu,
                bias=E["SCL"][:, 112 + _oc:113 + _oc]),
                r=("ps0", "SCL"), w=(f"H{oc}",))
        for ocd in range(8):
            for j2 in range(4):
                k.dma("q" if j2 % 2 == 0 else "g",
                      E["WS"][:, (j2 % 2) * 1024:(j2 % 2 + 1) * 1024].rearrange(
                          "p (a o) -> p a o", a=8),
                      io["ffw2T"][j2 * 1024:(j2 + 1) * 1024,
                                  ocd * 128:(ocd + 1) * 128].rearrange(
                          "(a p) o -> p a o", p=128), w=("WS",))
                k.op("v", lambda e, _j=j2: e.tensor_copy(
                    E["WB"][:, _j * 1024:(_j + 1) * 1024],
                    E["WS"][:, (_j % 2) * 1024:(_j % 2 + 1) * 1024]),
                    r=("WS",), w=("WB",))

            def mm_2(eng):
                for j3 in range(32):
                    inst = eng.matmul(
                        E["ps2"][0:128, 0:512], E["WB"][:, j3 * 128:(j3 + 1) * 128],
                        hslot(j3), start=(j3 == 0), stop=(j3 == 31))
                return inst
            k.multi("t", mm_2, r=("WB",) + htags, w=("ps2",))
            k.op("v", lambda e, _o=ocd: e.tensor_scalar(
                E["SC"][:, 0:512], E["ps2"][0:128, 0:512],
                E["SCL"][:, 160 + _o:161 + _o], None, ALU.add),
                r=("ps2", "SCL"), w=("SC",))
            k.dma("q", E["WS"][:, 0:512],
                  io["xts"][ocd * 128:(ocd + 1) * 128, tt * 512:(tt + 1) * 512],
                  r=("HBMH",), w=("WS",))
            k.op("v", lambda e: e.tensor_add(
                E["SC"][:, 0:512], E["SC"][:, 0:512], E["WS"][:, 0:512]),
                r=("SC", "WS"), w=("SC",))
            k.op("v", lambda e, _o=ocd, _tt=tt: e.tensor_add(
                xr[_o][:, _tt * 512:(_tt + 1) * 512],
                xr[_o][:, _tt * 512:(_tt + 1) * 512], E["SC"][:, 0:512]),
                r=("SC", f"XR{ocd}"), w=(f"XR{ocd}",))


def build_l3a2_program():
    nc = bass.Bass("TRN2", target_bir_lowering=False, debug=False,
                   detect_race_conditions=False)
    x_in = nc.dram_tensor("x_in", [D, TC], F32, kind="ExternalInput").ap()
    kon = nc.dram_tensor("kon", [128, 140], F32, kind="ExternalInput").ap()
    io = {}
    for n, shp in {"yf": [1024, 2048], "yb": [1024, 2048],
                   "ffw1T": [1024, 4096], "ffw2T": [4096, 1024],
                   "scl": [128, 224]}.items():
        io[n] = nc.dram_tensor(n, shp, F32, kind="ExternalInput").ap()
    io["xts"] = nc.dram_tensor("xts", [1024, 2048], F32, kind="Internal").ap()
    x_out = nc.dram_tensor("x_out", [D, TC], F32, kind="ExternalOutput").ap()
    es = ExitStack()
    k = K(nc)
    with es:
        k.start(es)
        E = alloc_env(k)
        emit_env_init(k, E, kon)
        emit_load_resid(k, E, x_in)
        emit_l3a2(k, E, io)
        emit_store_resid(k, E, x_out)
        k.finalize()
    return nc


def run_l3a(x_cores, inputs):
    ydirs = run_l3a1(x_cores, inputs)
    scl = np.zeros((128, 224), np.float32)
    scl[:, 112:144] = inputs["L3_attn_ff_b1"].reshape(32, 128).T
    scl[:, 144:152] = inputs["L3_attn_ff_ln_g"].reshape(8, 128).T
    scl[:, 152:160] = inputs["L3_attn_ff_ln_b"].reshape(8, 128).T
    scl[:, 160:168] = inputs["L3_attn_ff_b2"].reshape(8, 128).T
    host = {"ffw1T": _t(inputs["L3_attn_ff_w1"]), "ffw2T": _t(inputs["L3_attn_ff_w2"]),
            "scl": scl, "kon": konst_host()}
    nc = build_l3a2_program()
    in_maps = []
    for c in range(NCORE):
        b, h = c // 2, c % 2
        yf = ydirs[2 * b]
        yb = ydirs[2 * b + 1]
        m = dict(host)
        m["x_in"] = x_cores[c]
        m["yf"] = np.ascontiguousarray(yf[:, h * 2048:(h + 1) * 2048])
        m["yb"] = np.ascontiguousarray(yb[:, h * 2048:(h + 1) * 2048])
        in_maps.append(m)
    res = run_bass_kernel_spmd(nc, in_maps, core_ids=list(range(NCORE)))
    return [res.results[c]["x_out"] for c in range(NCORE)]
